# version 15
# baseline (speedup 1.0000x reference)
import numpy as np
from contextlib import ExitStack

import concourse.bass as bass
import concourse.mybir as mybir
from concourse.bass_utils import run_bass_kernel_spmd

F32 = mybir.dt.float32
BF16 = mybir.dt.bfloat16
AF = mybir.ActivationFunctionType
OP = mybir.AluOpType
AX = mybir.AxisListType

B, L, C, HF = 32, 2048, 1024, 4096
NCORES = 8
BPC = B // NCORES
NCH = C // 128
NHC = HF // 128
TT = 512
NTT = L // TT
LEPS = 1e-5
GEPS = 1e-6


class Prog:

    def __init__(self, nc):
        self.nc = nc
        self.ops = {"sync": [], "tensor": [], "vector": [], "scalar": [], "gpsimd": []}

    def op(self, eng, fn, ch=None, k=1):
        if ch is not None:
            ch.n += k
            self.ops[eng].append(lambda e, fn=fn, ch=ch, k=k: fn(e).then_inc(ch.sem, k))
            return ch.n
        self.ops[eng].append(fn)
        return None

    def wait(self, eng, ch, val=None):
        v = ch.n if val is None else val
        assert v <= ch.n, (eng, v, ch.n)
        self.ops[eng].append(lambda e, s=ch.sem, v=v: e.wait_ge(s, v))

    def dma(self, eng, pool, fn):
        ch = pool.take()
        if ch.n > 0:
            self.wait(eng, ch)
        n = self.op(eng, fn, ch=ch, k=16)
        return (ch, n)

    def wait_dma(self, eng, pool):
        for ch in pool.chans:
            if ch.n > 0:
                self.wait(eng, ch)

    def run(self):
        with self.nc.Block() as blk:
            for name in ("sync", "tensor", "vector", "scalar", "gpsimd"):
                ops = self.ops[name]
                if ops:
                    getattr(blk, name)(lambda eng, ops=tuple(ops): [f(eng) for f in ops])


class Chan:
    def __init__(self, stack, nc, name):
        self.sem = stack.enter_context(nc.semaphore(name))
        self.n = 0


class Pool:
    def __init__(self, stack, nc, name, width):
        self.chans = [Chan(stack, nc, f"{name}{i}") for i in range(width)]
        self.i = 0

    def take(self):
        ch = self.chans[self.i % len(self.chans)]
        self.i += 1
        return ch


def build_nc(dbg=False):
    nc = bass.Bass("TRN2", target_bir_lowering=False, detect_race_conditions=False)

    xin = nc.dram_tensor("xT", [BPC, C, L], F32, kind="ExternalInput").ap()
    wq = nc.dram_tensor("wq", [C, C], F32, kind="ExternalInput").ap()
    wk = nc.dram_tensor("wk", [C, C], F32, kind="ExternalInput").ap()
    wv = nc.dram_tensor("wv", [C, C], F32, kind="ExternalInput").ap()
    bq_d = nc.dram_tensor("bq", [1, C], F32, kind="ExternalInput").ap()
    bk_d = nc.dram_tensor("bk", [1, C], F32, kind="ExternalInput").ap()
    bv_d = nc.dram_tensor("bv", [1, C], F32, kind="ExternalInput").ap()
    gq_d = nc.dram_tensor("gq", [128, NCH], F32, kind="ExternalInput").ap()
    gk_d = nc.dram_tensor("gk", [128, NCH], F32, kind="ExternalInput").ap()
    bbq_d = nc.dram_tensor("betaq", [128, NCH], F32, kind="ExternalInput").ap()
    bbk_d = nc.dram_tensor("betak", [128, NCH], F32, kind="ExternalInput").ap()
    gv_d = nc.dram_tensor("gv", [1, C], F32, kind="ExternalInput").ap()
    bbv_d = nc.dram_tensor("betav", [1, C], F32, kind="ExternalInput").ap()
    wfcT = nc.dram_tensor("wfcT", [C, HF], F32, kind="ExternalInput").ap()
    wprojT = nc.dram_tensor("wprojT", [HF, C], F32, kind="ExternalInput").ap()
    lnw_d = nc.dram_tensor("lnw", [128, NHC], F32, kind="ExternalInput").ap()
    lnb_d = nc.dram_tensor("lnb", [128, NHC], F32, kind="ExternalInput").ap()
    ident_d = nc.dram_tensor("ident", [128, 128], F32, kind="ExternalInput").ap()
    onesc_d = nc.dram_tensor("onesc", [128, 2], F32, kind="ExternalInput").ap()
    onesc2_d = nc.dram_tensor("onesc2", [128, 2], F32, kind="ExternalInput").ap()
    onesrow_d = nc.dram_tensor("onesrow", [1, TT], F32, kind="ExternalInput").ap()

    outT = nc.dram_tensor("outT", [BPC, C, L], F32, kind="ExternalOutput").ap()

    _k = "ExternalOutput" if dbg else "Internal"
    xnh_sp = nc.dram_tensor("xnh_sp", [NTT, NCH, 128, TT], BF16, kind=_k).ap()
    xnl_sp = nc.dram_tensor("xnl_sp", [NTT, NCH, 128, TT], BF16, kind=_k).ap()
    qn_sp = nc.dram_tensor("qn_sp", [NTT, NCH, 128, 2 * TT], BF16, kind=_k).ap()
    kn_sp = nc.dram_tensor("kn_sp", [NTT, NCH, 128, 2 * TT], BF16, kind=_k).ap()
    o_sp = nc.dram_tensor("o_sp", [2 * NTT, NCH, 128, 256], F32, kind=_k).ap()
    vn_sp = nc.dram_tensor("vn_sp", [16, 128, 2 * C], BF16, kind=_k).ap()
    wqkv_sp = nc.dram_tensor("wqkv_sp", [3, NCH, 128, 2 * C], BF16, kind=_k).ap()
    wfcq_sp = nc.dram_tensor("wfcq_sp", [NHC, 128, NCH * 128], BF16, kind=_k).ap()
    wprojq_sp = nc.dram_tensor("wprojq_sp", [NHC, 128, C], BF16, kind=_k).ap()
    gvb_sp = nc.dram_tensor("gvb_sp", [2, 128, C], F32, kind=_k).ap()

    stack = ExitStack()
    ctx = stack.enter_context

    PIN = Pool(stack, nc, "cIN", 6)
    PSP = Pool(stack, nc, "cSP", 8)
    POUT = Pool(stack, nc, "cOUT", 2)
    cT = Chan(stack, nc, "cT")
    cV = Chan(stack, nc, "cV")
    cA = Chan(stack, nc, "cA")

    ident = ctx(nc.sbuf_tensor("ident_s", [128, 128], F32))
    onesc = ctx(nc.sbuf_tensor("onesc_s", [128, 2], F32))
    onesc2 = ctx(nc.sbuf_tensor("onesc2_s", [128, 2], F32))
    onesrow = ctx(nc.sbuf_tensor("onesrow_s", [1, TT], F32))
    gq_t = ctx(nc.sbuf_tensor("gq_t", [128, NCH], F32))
    gk_t = ctx(nc.sbuf_tensor("gk_t", [128, NCH], F32))
    bbq_t = ctx(nc.sbuf_tensor("bbq_t", [128, NCH], F32))
    bbk_t = ctx(nc.sbuf_tensor("bbk_t", [128, NCH], F32))
    lnw_t = ctx(nc.sbuf_tensor("lnw_t", [128, NHC], F32))
    lnb_t = ctx(nc.sbuf_tensor("lnb_t", [128, NHC], F32))
    st_row = ctx(nc.sbuf_tensor("st_row", [1, 2 * TT], F32))
    thr_p = ctx(nc.sbuf_tensor("thr_p", [128, 1], F32))
    nthr_p = ctx(nc.sbuf_tensor("nthr_p", [128, 1], F32))
    grow = ctx(nc.sbuf_tensor("grow", [1, 1], F32))
    PSA = ctx(nc.psum_tensor("psarena", [128, NCH * TT], F32))

    def quiesce(p, pools=("PIN", "PSP", "POUT")):
        pm = {"PIN": PIN, "PSP": PSP, "POUT": POUT}
        for eng in ("sync", "gpsimd", "vector", "scalar", "tensor"):
            for ch in (cT, cV, cA):
                if ch.n > 0:
                    p.wait(eng, ch)
            for pn in pools:
                p.wait_dma(eng, pm[pn])

    def vseq(p, fns):
        for fn in fns:
            p.wait("vector", cV)
            p.op("vector", fn, ch=cV)

    def ln_bcast(p, ps_st, ps_b, n_feat, br_t, bmr_t, sA, sB, nr=True):
        stats_free = p.op("vector", lambda v: v.tensor_copy(out=st_row[:], in_=ps_st[:]), ch=cV)
        p.wait("tensor", cV)
        p.op("tensor", lambda t: t.matmul(ps_b[:], onesrow[0:1, 0:128], st_row[0:1, 0:TT], start=True, stop=True), ch=cT)
        p.wait("vector", cT)
        p.op("vector", lambda v: v.tensor_scalar_mul(out=bmr_t[:], in0=ps_b[:], scalar1=1.0 / n_feat), ch=cV)
        p.wait("tensor", cV)
        p.op("tensor", lambda t: t.matmul(ps_b[:], onesrow[0:1, 0:128], st_row[0:1, TT:2 * TT], start=True, stop=True), ch=cT)
        p.wait("vector", cT)
        p.op("vector", lambda v: v.tensor_scalar_mul(out=sA[:], in0=ps_b[:], scalar1=1.0 / n_feat))
        p.op("vector", lambda v: v.tensor_mul(out=sB[:], in0=bmr_t[:], in1=bmr_t[:]))
        p.op("vector", lambda v: v.tensor_sub(out=sA[:], in0=sA[:], in1=sB[:]))
        p.op("vector", lambda v: v.tensor_scalar_add(out=sA[:], in0=sA[:], scalar1=LEPS), ch=cV)
        p.wait("scalar", cV)
        p.op("scalar", lambda s: s.activation(out=sB[:], in_=sA[:], func=AF.Sqrt), ch=cA)
        p.wait("vector", cA)
        if nr:
            p.op("vector", lambda v: v.reciprocal(out=sB[:], in_=sB[:]))
            p.op("vector", lambda v: v.tensor_mul(out=br_t[:], in0=sB[:], in1=sB[:]))
            p.op("vector", lambda v: v.tensor_mul(out=br_t[:], in0=br_t[:], in1=sA[:]))
            p.op("vector", lambda v: v.tensor_scalar(out=br_t[:], in0=br_t[:], scalar1=-0.5, scalar2=1.5, op0=OP.mult, op1=OP.add))
            p.op("vector", lambda v: v.tensor_mul(out=br_t[:], in0=sB[:], in1=br_t[:]))
        else:
            p.op("vector", lambda v: v.reciprocal(out=br_t[:], in_=sB[:]))
        p.op("vector", lambda v: v.tensor_mul(out=bmr_t[:], in0=bmr_t[:], in1=br_t[:]), ch=cV)
        return stats_free

    PG = Prog(nc)

    def _p0():
        with ExitStack() as ph:
            pc = ph.enter_context
            gv_r = pc(nc.sbuf_tensor("gv_r", [1, C], F32))
            bbv_r = pc(nc.sbuf_tensor("bbv_r", [1, C], F32))
            bgv = pc(nc.sbuf_tensor("bgv0", [128, C], F32))
            bbvB = pc(nc.sbuf_tensor("bbvB0", [128, C], F32))
            wld = pc(nc.sbuf_tensor("p0_wld", [128, 2 * HF], F32))
            am = pc(nc.sbuf_tensor("p0_am", [128, 2 * HF], BF16))
            bm = pc(nc.sbuf_tensor("p0_bm", [128, 2 * HF], BF16))
            wqs = pc(nc.sbuf_tensor("p0_wqs", [128, 2 * HF], BF16))
            acol = pc(nc.sbuf_tensor("p0_acol", [128, 1], F32))
            asum = pc(nc.sbuf_tensor("p0_asum", [128, 1], F32))
            ps_g = PSA[0:1, 7 * TT:7 * TT + 1]
            ps_b = PSA[:, 0:C]

            p = PG
            for src, dst in ((ident_d, ident), (onesc_d, onesc), (onesc2_d, onesc2),
                             (onesrow_d, onesrow), (gq_d, gq_t), (gk_d, gk_t),
                             (bbq_d, bbq_t), (bbk_d, bbk_t), (lnw_d, lnw_t), (lnb_d, lnb_t),
                             (gv_d, gv_r), (bbv_d, bbv_r)):
                p.dma("sync", PIN, lambda s, a=src, b=dst: s.dma_start(out=b[:], in_=a[:]))

            p.wait_dma("tensor", PIN)
            p.op("tensor", lambda t: t.matmul(ps_b[:, 0:TT], onesrow[0:1, 0:128], gv_r[0:1, 0:TT], start=True, stop=True))
            p.op("tensor", lambda t: t.matmul(ps_b[:, TT:C], onesrow[0:1, 0:128], gv_r[0:1, TT:C], start=True, stop=True), ch=cT)
            p.wait("vector", cT)
            p.op("vector", lambda v: v.tensor_copy(out=bgv[:], in_=ps_b[:]), ch=cV)
            p.wait("tensor", cV)
            p.op("tensor", lambda t: t.matmul(ps_b[:, 0:TT], onesrow[0:1, 0:128], bbv_r[0:1, 0:TT], start=True, stop=True))
            p.op("tensor", lambda t: t.matmul(ps_b[:, TT:C], onesrow[0:1, 0:128], bbv_r[0:1, TT:C], start=True, stop=True), ch=cT)
            p.wait("vector", cT)
            p.op("vector", lambda v: v.tensor_copy(out=bbvB[:], in_=ps_b[:]), ch=cV)
            p.wait("gpsimd", cV)
            p.dma("gpsimd", PSP, lambda g, bgv=bgv: g.dma_start(out=gvb_sp[0], in_=bgv[:]))
            p.dma("gpsimd", PSP, lambda g, bbvB=bbvB: g.dma_start(out=gvb_sp[1], in_=bbvB[:]))

            acols = pc(nc.sbuf_tensor("p0_acols", [128, NCH + NHC], F32))
            thr2_p = pc(nc.sbuf_tensor("p0_thr2", [128, 1], F32))
            nthr2_p = pc(nc.sbuf_tensor("p0_nthr2", [128, 1], F32))
            MATS = ((wfcT, NCH, HF, "fc", 0), (wprojT, NHC, C, "pj", NCH))
            red = {}
            gq = 0
            for (wsrc, nchunks, fchunk, tag, abase) in MATS:
                for ci in range(nchunks):
                    slot = (gq % 2) * HF
                    if gq >= 2:
                        p.wait("sync", cV, red[gq - 2])
                    t1 = p.dma("sync", PIN, lambda s, ci=ci, wsrc=wsrc, f=fchunk, slot=slot: s.dma_start(
                        out=wld[:, slot:slot + f // 2], in_=wsrc[ci * 128:(ci + 1) * 128, 0:f // 2]))
                    t2 = p.dma("sync", PIN, lambda s, ci=ci, wsrc=wsrc, f=fchunk, slot=slot: s.dma_start(
                        out=wld[:, slot + f // 2:slot + f], in_=wsrc[ci * 128:(ci + 1) * 128, f // 2:f]))
                    p.wait("vector", t1[0], t1[1])
                    p.wait("vector", t2[0], t2[1])
                    red[gq] = p.op("vector", lambda v, f=fchunk, slot=slot, ac=abase + ci: v.reduce_sum(
                        out=acols[:, ac:ac + 1], in_=wld[:, slot:slot + f], axis=AX.X,
                        apply_absolute_value=True), ch=cV)
                    gq += 1
            for (wsrc, nchunks, fchunk, tag, abase) in MATS:
                tp_, np_ = (thr_p, nthr_p) if tag == "fc" else (thr2_p, nthr2_p)
                p.wait("vector", cV)
                vseq(p, [lambda v, abase=abase, nchunks=nchunks: v.reduce_sum(
                    out=acol[:], in_=acols[:, abase:abase + nchunks], axis=AX.X)])
                p.wait("tensor", cV)
                p.op("tensor", lambda t: t.matmul(ps_g[:], onesc[:, 0:1], acol[:], start=True, stop=True), ch=cT)
                p.wait("vector", cT)
                vseq(p, [lambda v: v.tensor_scalar_mul(out=grow[:], in0=ps_g[:], scalar1=1.0 / (C * HF)),
                         lambda v: v.tensor_scalar_add(out=grow[:], in0=grow[:], scalar1=GEPS)])
                p.wait("tensor", cV)
                p.op("tensor", lambda t: t.matmul(ps_b[:, 0:1], onesrow[0:1, 0:128], grow[:], start=True, stop=True), ch=cT)
                p.wait("vector", cT)
                vseq(p, [lambda v, tp_=tp_: v.tensor_scalar_mul(out=tp_[:], in0=ps_b[:, 0:1], scalar1=0.5),
                         lambda v, tp_=tp_, np_=np_: v.tensor_scalar_mul(out=np_[:], in0=tp_[:], scalar1=-1.0)])
            qvv = {}
            sptoks = {}
            gq = 0
            for (wsrc, nchunks, fchunk, tag, abase) in MATS:
                tp_, np_ = (thr_p, nthr_p) if tag == "fc" else (thr2_p, nthr2_p)
                for ci in range(nchunks):
                    slot = (gq % 2) * HF
                    if gq < 2:
                        p.wait("sync", cV, red[38 + gq])
                    else:
                        p.wait("sync", cV, qvv[gq - 2])
                    t1 = p.dma("sync", PIN, lambda s, ci=ci, wsrc=wsrc, f=fchunk, slot=slot: s.dma_start(
                        out=wld[:, slot:slot + f // 2], in_=wsrc[ci * 128:(ci + 1) * 128, 0:f // 2]))
                    t2 = p.dma("sync", PIN, lambda s, ci=ci, wsrc=wsrc, f=fchunk, slot=slot: s.dma_start(
                        out=wld[:, slot + f // 2:slot + f], in_=wsrc[ci * 128:(ci + 1) * 128, f // 2:f]))
                    p.wait("vector", t1[0], t1[1])
                    p.wait("vector", t2[0], t2[1])
                    if gq >= 2:
                        for tok in sptoks[gq - 2]:
                            p.wait("vector", tok[0], tok[1])
                    p.op("vector", lambda v, f=fchunk, slot=slot, tp_=tp_: v.tensor_single_scalar(
                        out=am[:, slot:slot + f], in_=wld[:, slot:slot + f], scalar=tp_[:], op=OP.is_gt))
                    p.op("vector", lambda v, f=fchunk, slot=slot, np_=np_: v.tensor_single_scalar(
                        out=bm[:, slot:slot + f], in_=wld[:, slot:slot + f], scalar=np_[:], op=OP.is_lt))
                    qvv[gq] = p.op("vector", lambda v, f=fchunk, slot=slot: v.tensor_sub(
                        out=wqs[:, slot:slot + f], in0=am[:, slot:slot + f], in1=bm[:, slot:slot + f]), ch=cV)
                    p.wait("gpsimd", cV, qvv[gq])
                    toks = []
                    if tag == "fc":
                        for h in range(NHC):
                            toks.append(p.dma("gpsimd", PSP, lambda g, ci=ci, h=h, slot=slot: g.dma_start(
                                out=wfcq_sp[h][:, ci * 128:(ci + 1) * 128], in_=wqs[:, slot + h * 128:slot + (h + 1) * 128])))
                    else:
                        toks.append(p.dma("gpsimd", PSP, lambda g, ci=ci, slot=slot: g.dma_start(
                            out=wprojq_sp[ci], in_=wqs[:, slot:slot + C])))
                    sptoks[gq] = toks
                    gq += 1
            wsp = pc(nc.sbuf_tensor("p0_wsp", [128, 2 * 2 * C], BF16))
            wtoks = {}
            wsptoks = {}
            for wi, wsrc in enumerate((wq, wk, wv)):
                for ci in range(NCH):
                    gwi = wi * NCH + ci
                    half = slice((gwi % 2) * C, (gwi % 2) * C + C)
                    if gwi >= 2:
                        p.wait("sync", cV, wtoks[gwi - 2])
                    else:
                        p.wait("sync", cV)
                        p.wait("sync", cA)
                    ltok = p.dma("sync", PIN, lambda s, ci=ci, wsrc=wsrc, half=half: s.dma_start(
                        out=wld[:, half], in_=wsrc[ci * 128:(ci + 1) * 128, :]))
                    p.wait("vector", ltok[0], ltok[1])
                    wofs = (gwi % 2) * 2 * C
                    if gwi >= 2 and wsptoks.get(gwi - 2) is not None:
                        tok = wsptoks[gwi - 2]
                        p.wait("vector", tok[0], tok[1])
                    p.op("vector", lambda v, half=half, wofs=wofs: v.tensor_copy(
                        out=wsp[:, wofs:wofs + C], in_=wld[:, half]))
                    wtoks[gwi] = p.op("vector", lambda v, half=half, wofs=wofs: v.tensor_sub(
                        out=wsp[:, wofs + C:wofs + 2 * C], in0=wld[:, half], in1=wsp[:, wofs:wofs + C]), ch=cV)
                    p.wait("gpsimd", cV, wtoks[gwi])
                    wsptoks[gwi] = p.dma("gpsimd", PSP, lambda g, wi=wi, ci=ci, wofs=wofs: g.dma_start(
                        out=wqkv_sp[wi, ci], in_=wsp[:, wofs:wofs + 2 * C]))

    _p0()
    for b in range(BPC):
        s_xn = ExitStack()
        xnh_sb = s_xn.enter_context(nc.sbuf_tensor(f"xnh_{b}", [128, NCH * L], BF16))
        xnl_sb = s_xn.enter_context(nc.sbuf_tensor(f"xnl_{b}", [128, NCH * L], BF16))
        w_sb = s_xn.enter_context(nc.sbuf_tensor(f"w_{b}", [128, 2 * NCH * 2 * C], BF16))
        wtok_h = {}

        def load_w(p, wi, wbuf, gate=None, w_sb=w_sb):
            if gate is not None:
                p.wait("sync", cT, gate)
            toks = []
            for ci in range(NCH):
                toks.append(p.dma("sync", PIN, lambda s, ci=ci, wi=wi, wbuf=wbuf, w_sb=w_sb: s.dma_start(
                    out=w_sb[:, wbuf * NCH * 2 * C + ci * 2 * C: wbuf * NCH * 2 * C + (ci + 1) * 2 * C],
                    in_=wqkv_sp[wi, ci])))
            return toks

        def _ph1(b=b, xnh_sb=xnh_sb, xnl_sb=xnl_sb, load_w=load_w, wtok_h=wtok_h):
            with ExitStack() as ph:
                pc = ph.enter_context
                xt_sb = pc(nc.sbuf_tensor(f"xt_{b}", [128, 2 * NCH * TT], F32))
                x2s = pc(nc.sbuf_tensor(f"x2s_{b}", [128, 2 * TT], F32))
                xw = pc(nc.sbuf_tensor(f"xw_{b}", [128, TT], F32))
                br_t = pc(nc.sbuf_tensor(f"br1_{b}", [128, TT], F32))
                bmr_t = pc(nc.sbuf_tensor(f"bmr1_{b}", [128, TT], F32))
                sA = pc(nc.sbuf_tensor(f"sA1_{b}", [128, TT], F32))
                sB = pc(nc.sbuf_tensor(f"sB1_{b}", [128, TT], F32))
                ps_st = PSA[0:1, 0:2 * TT]
                ps_b1 = PSA[:, 2 * TT:3 * TT]

                p = PG
                quiesce(p)
                xtoks = {}

                def load_xt(ti):
                    xb = (ti % 2) * NCH * TT
                    for ci in range(NCH):
                        xtoks[(ti, ci)] = p.dma("sync", PIN, lambda s, ci=ci, ti=ti, xb=xb: s.dma_start(
                            out=xt_sb[:, xb + ci * TT: xb + (ci + 1) * TT],
                            in_=xin[b, ci * 128:(ci + 1) * 128, ti * TT:(ti + 1) * TT]))

                load_xt(0)
                load_xt(1)
                wtok_h["q"] = load_w(p, 0, 0)
                wtok_h["k"] = load_w(p, 1, 1)
                stats_free = None
                sqv = {}
                st2 = {}
                xuse = {}
                for ti in range(NTT):
                    if stats_free is not None:
                        p.wait("tensor", cV, stats_free)
                    for ci in range(NCH):
                        gi = ti * NCH + ci
                        xb = (ti % 2) * NCH * TT
                        sl = slice(xb + ci * TT, xb + (ci + 1) * TT)
                        nsl = slice(ci * L + ti * TT, ci * L + ti * TT + TT)
                        buf = slice((ci % 2) * TT, (ci % 2) * TT + TT)
                        tok = xtoks[(ti, ci)]
                        p.wait("scalar", tok[0], tok[1])
                        p.wait("tensor", tok[0], tok[1])
                        if gi >= 2:
                            p.wait("scalar", cT, st2[gi - 2])
                        sqv[gi] = p.op("scalar", lambda s, sl=sl, buf=buf: s.activation(
                            out=x2s[:, buf], in_=xt_sb[:, sl], func=AF.Square), ch=cA)
                        p.op("tensor", lambda t, sl=sl, ci=ci: t.matmul(
                            ps_st[0:1, 0:TT], onesc[:, 0:1], xt_sb[:, sl], start=(ci == 0), stop=(ci == NCH - 1)))
                        p.wait("tensor", cA, sqv[gi])
                        st2[gi] = p.op("tensor", lambda t, buf=buf, ci=ci: t.matmul(
                            ps_st[0:1, TT:2 * TT], onesc[:, 0:1], x2s[:, buf], start=(ci == 0), stop=(ci == NCH - 1)), ch=cT)
                    p.wait("vector", cT, st2[ti * NCH + NCH - 1])
                    stats_free = ln_bcast(p, ps_st, ps_b1, C, br_t, bmr_t, sA, sB)
                    p.wait("vector", cV)
                    for ci in range(NCH):
                        xb = (ti % 2) * NCH * TT
                        sl = slice(xb + ci * TT, xb + (ci + 1) * TT)
                        nsl = slice(ci * L + ti * TT, ci * L + ti * TT + TT)
                        p.op("vector", lambda v, sl=sl: v.tensor_mul(out=xw[:], in0=xt_sb[:, sl], in1=br_t[:]))
                        p.op("vector", lambda v: v.tensor_sub(out=xw[:], in0=xw[:], in1=bmr_t[:]))
                        p.op("vector", lambda v, nsl=nsl: v.tensor_copy(out=xnh_sb[:, nsl], in_=xw[:]))
                        xv = p.op("vector", lambda v, nsl=nsl: v.tensor_sub(out=xnl_sb[:, nsl], in0=xw[:], in1=xnh_sb[:, nsl]), ch=cV)
                        p.wait("gpsimd", cV, xv)
                        p.dma("gpsimd", PSP, lambda g, ci=ci, ti=ti, nsl=nsl: g.dma_start(
                            out=xnh_sp[ti, ci], in_=xnh_sb[:, nsl]))
                        p.dma("gpsimd", PSP, lambda g, ci=ci, ti=ti, nsl=nsl: g.dma_start(
                            out=xnl_sp[ti, ci], in_=xnl_sb[:, nsl]))
                    xuse[ti] = (cT.n, cV.n, cA.n)
                    if ti + 2 < NTT:
                        p.wait("sync", cT, xuse[ti][0])
                        p.wait("sync", cV, xuse[ti][1])
                        p.wait("sync", cA, xuse[ti][2])
                        load_xt(ti + 2)
    
        _ph1()
        def _ph2(b=b, xnh_sb=xnh_sb, xnl_sb=xnl_sb, w_sb=w_sb, load_w=load_w, wtok_h=wtok_h):
            with ExitStack() as ph:
                pc = ph.enter_context
                y_tile = pc(nc.sbuf_tensor(f"yt_{b}", [128, NCH * TT], F32))
                y2s = pc(nc.sbuf_tensor(f"y2s_{b}", [128, 2 * TT], F32))
                br_t = pc(nc.sbuf_tensor(f"br2_{b}", [128, TT], F32))
                bmr_t = pc(nc.sbuf_tensor(f"bmr2_{b}", [128, TT], F32))
                sA = pc(nc.sbuf_tensor(f"sA2_{b}", [128, TT], F32))
                sB = pc(nc.sbuf_tensor(f"sB2_{b}", [128, TT], F32))
                qn_out = pc(nc.sbuf_tensor(f"qno_{b}", [128, TT], F32))
                qs_out = pc(nc.sbuf_tensor(f"qso_{b}", [128, 5 * 2 * TT], BF16))
                c_mu = pc(nc.sbuf_tensor(f"cmu_{b}", [128, 1], F32))
                c_msq = pc(nc.sbuf_tensor(f"cmsq_{b}", [128, 1], F32))
                c_t0 = pc(nc.sbuf_tensor(f"ct0_{b}", [128, 1], F32))
                c_a = pc(nc.sbuf_tensor(f"ca_{b}", [128, 1], F32))
                c_r = pc(nc.sbuf_tensor(f"cr_{b}", [128, 1], F32))
                s1c = pc(nc.sbuf_tensor(f"s1c_{b}", [128, 1], F32))
                s2c = pc(nc.sbuf_tensor(f"s2c_{b}", [128, 1], F32))
                ps_y = PSA[:, 0:2 * TT]
                ps_st = PSA[0:1, 2 * TT:4 * TT]
                ps_b1 = PSA[:, 4 * TT:5 * TT]
                ps_v = PSA[:, 5 * TT:7 * TT]
                vnob = pc(nc.sbuf_tensor(f"vno_{b}", [128, C], F32))
                vns = pc(nc.sbuf_tensor(f"vns_{b}", [128, 2 * 2 * C], BF16))
                bq_r = pc(nc.sbuf_tensor(f"bq_{b}", [1, C], F32))
                bk_r = pc(nc.sbuf_tensor(f"bk_{b}", [1, C], F32))
                bv_r = pc(nc.sbuf_tensor(f"bv_{b}", [1, C], F32))
                bgv = pc(nc.sbuf_tensor(f"bgv_{b}", [128, C], F32))
                bbvB = pc(nc.sbuf_tensor(f"bbvB_{b}", [128, C], F32))
                v_sbs = (y_tile[:, 0:C], y_tile[:, C:2 * C])
                v2ss = (y_tile[:, 2 * C:3 * C], y_tile[:, 3 * C:4 * C])

                p = PG
                quiesce(p, pools=("PIN",))
                for src_, dst_ in ((bq_d, bq_r), (bk_d, bk_r), (bv_d, bv_r)):
                    btok = p.dma("sync", PIN, lambda s, src_=src_, dst_=dst_: s.dma_start(out=dst_[:], in_=src_[:]))
                    p.wait("tensor", btok[0], btok[1])
                for gi_, dst_ in ((0, bgv), (1, bbvB)):
                    btok = p.dma("sync", PIN, lambda s, gi_=gi_, dst_=dst_: s.dma_start(out=dst_[:], in_=gvb_sp[gi_]))
                    p.wait("vector", btok[0], btok[1])
                last_w_use = [None]
                evac = {}
                sqv = {}
                st2 = {}
                spill = {}
                apply_v = {}
                gbase = [0]

                def proj_ln(p, wbuf, wtoks, b_row, g_tab, bb_tab, dst_sp):
                    for tok in wtoks:
                        p.wait("tensor", tok[0], tok[1])
                    wofs = wbuf * NCH * 2 * C
                    stats_free = None
                    g0 = gbase[0]
                    gbase[0] += NTT * NCH
                    for ti in range(NTT):
                        if stats_free is not None:
                            p.wait("tensor", cV, stats_free)
                        for h in range(NCH):
                            gi = g0 + ti * NCH + h
                            pb = slice((h % 2) * TT, (h % 2) * TT + TT)
                            if gi >= 2:
                                p.wait("tensor", cA, evac[gi - 2])
                            for ci in range(NCH):
                                xh = slice(ci * L + ti * TT, ci * L + ti * TT + TT)
                                whi = slice(wofs + ci * 2 * C + h * 128, wofs + ci * 2 * C + h * 128 + 128)
                                wlo = slice(wofs + ci * 2 * C + C + h * 128, wofs + ci * 2 * C + C + h * 128 + 128)
                                p.op("tensor", lambda t, pb=pb, whi=whi, xh=xh, ci=ci: t.matmul(
                                    ps_y[:, pb], w_sb[:, whi], xnh_sb[:, xh], start=(ci == 0), stop=False))
                                p.op("tensor", lambda t, pb=pb, whi=whi, xh=xh: t.matmul(
                                    ps_y[:, pb], w_sb[:, whi], xnl_sb[:, xh], start=False, stop=False))
                                p.op("tensor", lambda t, pb=pb, wlo=wlo, xh=xh: t.matmul(
                                    ps_y[:, pb], w_sb[:, wlo], xnh_sb[:, xh], start=False, stop=False))
                            bias_val = p.op("tensor", lambda t, pb=pb, h=h: t.matmul(
                                ps_y[:, pb], b_row[0:1, h * 128:(h + 1) * 128], onesrow[:], start=False, stop=True), ch=cT)
                            last_w_use[0] = bias_val
                            p.wait("scalar", cT, bias_val)
                            if gi >= NCH:
                                p.wait("scalar", cV, apply_v[gi - NCH])
                            ysl = slice(h * TT, (h + 1) * TT)
                            evac[gi] = p.op("scalar", lambda s, ysl=ysl, pb=pb: s.copy(out=y_tile[:, ysl], in_=ps_y[:, pb]), ch=cA)
                            y2b = slice((gi % 2) * TT, (gi % 2) * TT + TT)
                            if gi >= 2:
                                p.wait("scalar", cT, st2[gi - 2])
                            sqv[gi] = p.op("scalar", lambda s, ysl=ysl, y2b=y2b: s.activation(
                                out=y2s[:, y2b], in_=y_tile[:, ysl], func=AF.Square), ch=cA)
                            p.wait("tensor", cA, evac[gi])
                            p.op("tensor", lambda t, ysl=ysl, h=h: t.matmul(
                                ps_st[0:1, 0:TT], onesc[:, 0:1], y_tile[:, ysl], start=(h == 0), stop=(h == NCH - 1)))
                            p.wait("tensor", cA, sqv[gi])
                            st2[gi] = p.op("tensor", lambda t, y2b=y2b, h=h: t.matmul(
                                ps_st[0:1, TT:2 * TT], onesc[:, 0:1], y2s[:, y2b], start=(h == 0), stop=(h == NCH - 1)), ch=cT)
                        p.wait("vector", cT, st2[g0 + ti * NCH + NCH - 1])
                        stats_free = ln_bcast(p, ps_st, ps_b1, C, br_t, bmr_t, sA, sB)
                        p.wait("vector", cV)
                        for h in range(NCH):
                            gi = g0 + ti * NCH + h
                            ysl = slice(h * TT, (h + 1) * TT)
                            so = gi % 5
                            shi = slice(so * 2 * TT, so * 2 * TT + TT)
                            slo = slice(so * 2 * TT + TT, so * 2 * TT + 2 * TT)
                            if gi >= 5 and spill.get(gi - 5) is not None:
                                tok = spill[gi - 5]
                                p.wait("vector", tok[0], tok[1])
                            apply_v[gi] = p.op("vector", lambda v, ysl=ysl: v.tensor_mul(out=qn_out[:], in0=y_tile[:, ysl], in1=br_t[:]), ch=cV)
                            p.op("vector", lambda v: v.tensor_sub(out=qn_out[:], in0=qn_out[:], in1=bmr_t[:]))
                            p.op("vector", lambda v, h=h, g_tab=g_tab, bb_tab=bb_tab: v.tensor_scalar(
                                out=qn_out[:], in0=qn_out[:], scalar1=g_tab[:, h:h + 1],
                                scalar2=bb_tab[:, h:h + 1], op0=OP.mult, op1=OP.add))
                            p.op("vector", lambda v, shi=shi: v.tensor_copy(out=qs_out[:, shi], in_=qn_out[:]))
                            av = p.op("vector", lambda v, shi=shi, slo=slo: v.tensor_sub(
                                out=qs_out[:, slo], in0=qn_out[:], in1=qs_out[:, shi]), ch=cV)
                            p.wait("gpsimd", cV, av)
                            spill[gi] = p.dma("gpsimd", PSP, lambda g, ti=ti, h=h, so=so, dst_sp=dst_sp: g.dma_start(
                                out=dst_sp[ti, h], in_=qs_out[:, so * 2 * TT:(so + 1) * 2 * TT]))

                q_toks = wtok_h["q"]
                k_toks = wtok_h["k"]
                proj_ln(p, 0, q_toks, bq_r, gq_t, bbq_t, qn_sp)
                q_last = last_w_use[0]
                v_toks = load_w(p, 2, 0, gate=q_last)
                proj_ln(p, 1, k_toks, bk_r, gk_t, bbk_t, kn_sp)

                for tok in v_toks:
                    p.wait("tensor", tok[0], tok[1])
                wofs = 0
                vevac = {}
                vspill = {}
                vnv_v = {}
                for tt in range(16):
                    if tt > 0:
                        p.wait("tensor", cA, vevac[tt - 1])
                    for half in range(2):
                        hsl = slice(half * TT, (half + 1) * TT)
                        for ci in range(NCH):
                            xh = slice(ci * L + tt * 128, ci * L + tt * 128 + 128)
                            whi = slice(wofs + ci * 2 * C + half * TT, wofs + ci * 2 * C + half * TT + TT)
                            wlo = slice(wofs + ci * 2 * C + C + half * TT, wofs + ci * 2 * C + C + half * TT + TT)
                            p.op("tensor", lambda t, hsl=hsl, xh=xh, whi=whi, ci=ci: t.matmul(
                                ps_v[:, hsl], xnh_sb[:, xh], w_sb[:, whi], start=(ci == 0), stop=False))
                            p.op("tensor", lambda t, hsl=hsl, xh=xh, wlo=wlo: t.matmul(
                                ps_v[:, hsl], xnh_sb[:, xh], w_sb[:, wlo], start=False, stop=False))
                            p.op("tensor", lambda t, hsl=hsl, xh=xh, whi=whi: t.matmul(
                                ps_v[:, hsl], xnl_sb[:, xh], w_sb[:, whi], start=False, stop=False))
                        vmm = p.op("tensor", lambda t, hsl=hsl: t.matmul(
                            ps_v[:, hsl], onesrow[0:1, 0:128], bv_r[0:1, hsl], start=False, stop=True),
                            ch=(cT if half == 1 else None))
                    v_sb = v_sbs[tt % 2]
                    v2s = v2ss[tt % 2]
                    p.wait("scalar", cT, vmm)
                    if tt > 1:
                        p.wait("scalar", cV, vnv_v[tt - 2])
                    elif tt == 0:
                        p.wait("scalar", cV, apply_v[gbase[0] - 1])
                    vevac[tt] = p.op("scalar", lambda s, v_sb=v_sb: s.copy(out=v_sb[:], in_=ps_v[:]), ch=cA)
                    sqval = p.op("scalar", lambda s, v_sb=v_sb, v2s=v2s: s.activation(out=v2s[:], in_=v_sb[:], func=AF.Square), ch=cA)
                    p.wait("vector", cA, sqval)
                    vseq(p, [lambda v, v_sb=v_sb: v.reduce_sum(out=s1c[:], in_=v_sb[:], axis=AX.X),
                             lambda v, v2s=v2s: v.reduce_sum(out=s2c[:], in_=v2s[:], axis=AX.X),
                             lambda v: v.tensor_scalar_mul(out=c_mu[:], in0=s1c[:], scalar1=1.0 / C),
                             lambda v: v.tensor_mul(out=c_t0[:], in0=c_mu[:], in1=c_mu[:]),
                             lambda v: v.tensor_scalar(out=c_a[:], in0=s2c[:], scalar1=1.0 / C, scalar2=LEPS, op0=OP.mult, op1=OP.add),
                             lambda v: v.tensor_sub(out=c_a[:], in0=c_a[:], in1=c_t0[:])])
                    p.wait("scalar", cV)
                    p.op("scalar", lambda s: s.activation(out=c_t0[:], in_=c_a[:], func=AF.Sqrt), ch=cA)
                    p.wait("vector", cA)
                    vseq(p, [lambda v: v.reciprocal(out=c_r[:], in_=c_t0[:])])
                    p.wait("vector", cV)
                    p.op("vector", lambda v, v_sb=v_sb: v.tensor_scalar(out=v_sb[:], in0=v_sb[:], scalar1=c_mu[:], scalar2=c_r[:], op0=OP.subtract, op1=OP.mult))
                    p.op("vector", lambda v, v_sb=v_sb: v.tensor_mul(out=vnob[:], in0=v_sb[:], in1=bgv[:]))
                    p.op("vector", lambda v: v.tensor_add(out=vnob[:], in0=vnob[:], in1=bbvB[:]))
                    so = tt % 2
                    vhi = slice(so * 2 * C, so * 2 * C + C)
                    vlo = slice(so * 2 * C + C, so * 2 * C + 2 * C)
                    if tt >= 2 and vspill.get(tt - 2) is not None:
                        tok = vspill[tt - 2]
                        p.wait("vector", tok[0], tok[1])
                    p.op("vector", lambda v, vhi=vhi: v.tensor_copy(out=vns[:, vhi], in_=vnob[:]))
                    vnv = p.op("vector", lambda v, vhi=vhi, vlo=vlo: v.tensor_sub(out=vns[:, vlo], in0=vnob[:], in1=vns[:, vhi]), ch=cV)
                    vnv_v[tt] = vnv
                    p.wait("gpsimd", cV, vnv)
                    vspill[tt] = p.dma("gpsimd", PSP, lambda g, tt=tt, so=so: g.dma_start(
                        out=vn_sp[tt], in_=vns[:, so * 2 * C:(so + 1) * 2 * C]))
        _ph2()
        s_xn.close()

        SW = 256
        NSTR = L // SW
        def _ph3(b=b):
            with ExitStack() as ph:
                pc = ph.enter_context
                qn_sb = pc(nc.sbuf_tensor(f"qn_{b}", [128, NCH * NTT * 2 * TT], BF16))
                vn_sb = pc(nc.sbuf_tensor(f"vn_{b}", [128, 16 * 2 * C], BF16))
                kn_sb = pc(nc.sbuf_tensor(f"kn_{b}", [128, 2 * NCH * 2 * TT], BF16))
                s_sb = pc(nc.sbuf_tensor(f"ssb_{b}", [128, 2 * L], F32))
                at_sb = pc(nc.sbuf_tensor(f"at_{b}", [128, 16 * 2 * SW], BF16))
                o_buf = pc(nc.sbuf_tensor(f"ob_{b}", [128, 2 * SW], F32))
                nm_c = pc(nc.sbuf_tensor(f"nm_{b}", [128, 1], F32))
                sg_c = pc(nc.sbuf_tensor(f"sg_{b}", [128, 1], F32))
                rs_c = pc(nc.sbuf_tensor(f"rs_{b}", [128, 1], F32))
                ps_s = PSA[:, 0:2 * TT]
                ps_t = PSA[:, 2 * TT:4 * TT]
                ps_o = PSA[:, 4 * TT:6 * TT]

                p = PG
                quiesce(p)
                qn_toks = {}
                for ti in range(NTT):
                    toks = []
                    for h in range(NCH):
                        toks.append(p.dma("sync", PIN, lambda s, ti=ti, h=h: s.dma_start(
                            out=qn_sb[:, (h * NTT + ti) * 2 * TT: (h * NTT + ti + 1) * 2 * TT], in_=qn_sp[ti, h])))
                    qn_toks[ti] = toks
                vn_toks = []
                for tt in range(16):
                    vn_toks.append(p.dma("sync", PIN, lambda s, tt=tt: s.dma_start(
                        out=vn_sb[:, tt * 2 * C:(tt + 1) * 2 * C], in_=vn_sp[tt])))

                def load_kn(st_i):
                    kofs = (st_i % 2) * NCH * 2 * SW
                    toks = []
                    for h in range(NCH):
                        for piece in range(2):
                            toks.append(p.dma("sync", PIN, lambda s, st_i=st_i, h=h, piece=piece, kofs=kofs: s.dma_start(
                                out=kn_sb[:, kofs + h * 2 * SW + piece * SW: kofs + h * 2 * SW + (piece + 1) * SW],
                                in_=kn_sp[st_i // 2, h][:, piece * TT + (st_i % 2) * SW: piece * TT + (st_i % 2) * SW + SW])))
                    return toks

                kn_toks = {}
                smm = {}
                sev = {}
                norm_val = {}
                tp = {}
                tpev = {}
                ospill = {}
                oevac = {}
                last_score = None
                tp_emitted = set()

                def emit_transposes(gls):
                    if gls in tp_emitted:
                        return
                    tp_emitted.add(gls)
                    sbase = (gls % 2) * L
                    p.wait("tensor", cV, norm_val[gls])
                    for Lc in range(16):
                        tb = slice((Lc % 2) * TT, (Lc % 2) * TT + 128)
                        if (gls, Lc - 2) in tpev:
                            p.wait("tensor", cV, tpev[(gls, Lc - 2)])
                        elif Lc < 2 and (gls - 1, 14 + Lc) in tpev:
                            p.wait("tensor", cV, tpev[(gls - 1, 14 + Lc)])
                        tp[(gls, Lc)] = p.op("tensor", lambda t, tb=tb, Lc=Lc, sbase=sbase: t.transpose(
                            ps_t[:, tb], s_sb[:, sbase + Lc * 128: sbase + (Lc + 1) * 128], ident[:]), ch=cT)
                        p.wait("vector", cT, tp[(gls, Lc)])
                        ls = gls % 2
                        ahi = slice(Lc * 2 * SW + ls * 128, Lc * 2 * SW + ls * 128 + 128)
                        alo = slice(Lc * 2 * SW + SW + ls * 128, Lc * 2 * SW + SW + ls * 128 + 128)
                        p.op("vector", lambda v, ahi=ahi, tb=tb: v.tensor_copy(out=at_sb[:, ahi], in_=ps_t[:, tb]))
                        tpev[(gls, Lc)] = p.op("vector", lambda v, ahi=ahi, alo=alo, tb=tb: v.tensor_sub(
                            out=at_sb[:, alo], in0=ps_t[:, tb], in1=at_sb[:, ahi]), ch=cV)

                kn_toks[0] = load_kn(0)
                for st_i in range(NSTR):
                    for tok in kn_toks[st_i]:
                        p.wait("tensor", tok[0], tok[1])
                    kofs = (st_i % 2) * NCH * 2 * SW
                    for ls in range(SW // 128):
                        gls = st_i * 2 + ls
                        sbase = (gls % 2) * L
                        for Lt in range(NTT):
                            gmi = gls * 4 + Lt
                            pb = slice((Lt % 2) * TT, (Lt % 2) * TT + TT)
                            if gmi >= 2 and (gmi - 2) in sev:
                                p.wait("tensor", cA, sev[gmi - 2])
                            if st_i == 0 and ls == 0:
                                for tok in qn_toks[Lt]:
                                    p.wait("tensor", tok[0], tok[1])
                            for h in range(NCH):
                                khi = slice(kofs + h * 2 * SW + ls * 128, kofs + h * 2 * SW + ls * 128 + 128)
                                klo = slice(kofs + h * 2 * SW + SW + ls * 128, kofs + h * 2 * SW + SW + ls * 128 + 128)
                                qbase = (h * NTT + Lt) * 2 * TT
                                qhi = slice(qbase, qbase + TT)
                                qlo = slice(qbase + TT, qbase + 2 * TT)
                                p.op("tensor", lambda t, pb=pb, khi=khi, qhi=qhi, h=h: t.matmul(
                                    ps_s[:, pb], kn_sb[:, khi], qn_sb[:, qhi], start=(h == 0), stop=False))
                                p.op("tensor", lambda t, pb=pb, khi=khi, qlo=qlo: t.matmul(
                                    ps_s[:, pb], kn_sb[:, khi], qn_sb[:, qlo], start=False, stop=False))
                                smm[(gls, Lt)] = p.op("tensor", lambda t, pb=pb, klo=klo, qhi=qhi, h=h: t.matmul(
                                    ps_s[:, pb], kn_sb[:, klo], qn_sb[:, qhi], start=False, stop=(h == NCH - 1)),
                                    ch=(cT if h == NCH - 1 else None))
                            p.wait("scalar", cT, smm[(gls, Lt)])
                            if gls >= 2 and (gls - 2, 15) in tp:
                                p.wait("scalar", cT, tp[(gls - 2, 15)])
                            sev[gmi] = p.op("scalar", lambda s, Lt=Lt, pb=pb, sbase=sbase: s.activation(
                                out=s_sb[:, sbase + Lt * TT: sbase + (Lt + 1) * TT], in_=ps_s[:, pb], func=AF.Copy), ch=cA)
                        last_score = smm[(gls, 3)]
                        p.wait("vector", cA, sev[gls * 4 + 3])
                        nmv = p.op("vector", lambda v, sbase=sbase: v.reduce_max(out=nm_c[:], in_=s_sb[:, sbase:sbase + L], axis=AX.X, negate=True), ch=cV)
                        p.wait("scalar", cV, nmv)
                        ev = p.op("scalar", lambda s, sbase=sbase: s.activation(out=s_sb[:, sbase:sbase + L], in_=s_sb[:, sbase:sbase + L],
                                                                                func=AF.Exp, bias=nm_c[:]), ch=cA)
                        p.wait("vector", cA, ev)
                        vseq(p, [lambda v, sbase=sbase: v.reduce_sum(out=sg_c[:], in_=s_sb[:, sbase:sbase + L], axis=AX.X),
                                 lambda v: v.reciprocal(out=rs_c[:], in_=sg_c[:]),
                                 lambda v, sbase=sbase: v.tensor_scalar_mul(out=s_sb[:, sbase:sbase + L], in0=s_sb[:, sbase:sbase + L], scalar1=rs_c[:])])
                        norm_val[gls] = cV.n
                        if gls >= 1:
                            emit_transposes(gls - 1)
                    gls_last = st_i * 2 + 1
                    emit_transposes(gls_last)
                    if st_i + 1 < NSTR:
                        p.wait("sync", cT, last_score)
                        kn_toks[st_i + 1] = load_kn(st_i + 1)
                    p.wait("tensor", cV, tpev[(gls_last, 15)])
                    if st_i == 0:
                        for tok in vn_toks:
                            p.wait("tensor", tok[0], tok[1])
                    for cc in range(NCH):
                        pb = slice((cc % 2) * TT, (cc % 2) * TT + SW)
                        if cc >= 2:
                            p.wait("tensor", cV, oevac[cc - 2])
                        for Lc in range(16):
                            vbase = Lc * 2 * C
                            vhi = slice(vbase + cc * 128, vbase + cc * 128 + 128)
                            vlo = slice(vbase + C + cc * 128, vbase + C + cc * 128 + 128)
                            ahi = slice(Lc * 2 * SW, Lc * 2 * SW + SW)
                            alo = slice(Lc * 2 * SW + SW, Lc * 2 * SW + 2 * SW)
                            p.op("tensor", lambda t, pb=pb, vhi=vhi, ahi=ahi, Lc=Lc: t.matmul(
                                ps_o[:, pb], vn_sb[:, vhi], at_sb[:, ahi], start=(Lc == 0), stop=False))
                            p.op("tensor", lambda t, pb=pb, vhi=vhi, alo=alo: t.matmul(
                                ps_o[:, pb], vn_sb[:, vhi], at_sb[:, alo], start=False, stop=False))
                            avm = p.op("tensor", lambda t, pb=pb, vlo=vlo, ahi=ahi, Lc=Lc: t.matmul(
                                ps_o[:, pb], vn_sb[:, vlo], at_sb[:, ahi], start=False, stop=(Lc == 15)),
                                ch=(cT if Lc == 15 else None))
                        p.wait("vector", cT, avm)
                        ob = slice((cc % 2) * SW, (cc % 2) * SW + SW)
                        gcc = st_i * NCH + cc
                        if gcc >= 2 and ospill.get(gcc - 2) is not None:
                            tok = ospill[gcc - 2]
                            p.wait("vector", tok[0], tok[1])
                        oevac[cc] = p.op("vector", lambda v, ob=ob, pb=pb: v.tensor_copy(out=o_buf[:, ob], in_=ps_o[:, pb]), ch=cV)
                        p.wait("gpsimd", cV, oevac[cc])
                        ospill[gcc] = p.dma("gpsimd", PSP, lambda g, st_i=st_i, cc=cc, ob=ob: g.dma_start(
                            out=o_sp[st_i, cc], in_=o_buf[:, ob]))
    
        _ph3()
        def _ph4(b=b):
            with ExitStack() as ph:
                pc = ph.enter_context
                wpj_t = pc(nc.sbuf_tensor(f"wpj_{b}", [128, 2 * C], BF16))
                g_sb = pc(nc.sbuf_tensor(f"g_{b}", [128, NHC * TT], F32))
                wfct = pc(nc.sbuf_tensor(f"wfct_{b}", [128, 2 * NCH * 128], BF16))
                xnt = pc(nc.sbuf_tensor(f"xnt_{b}", [128, NCH * TT], F32))
                xnh = pc(nc.sbuf_tensor(f"xnh4_{b}", [128, 2 * NCH * TT], BF16))
                xnl = pc(nc.sbuf_tensor(f"xnl4_{b}", [128, 2 * NCH * TT], BF16))
                g2s = pc(nc.sbuf_tensor(f"g2s_{b}", [128, 2 * TT], F32))
                zs = pc(nc.sbuf_tensor(f"zs_{b}", [128, 2 * TT], F32))
                hn32 = pc(nc.sbuf_tensor(f"hn32_{b}", [128, 2 * TT], F32))
                hnh = pc(nc.sbuf_tensor(f"hnh_{b}", [128, 2 * TT], BF16))
                hnl = pc(nc.sbuf_tensor(f"hnl_{b}", [128, 2 * TT], BF16))
                br_t = pc(nc.sbuf_tensor(f"br4_{b}", [128, TT], F32))
                bmr_t = pc(nc.sbuf_tensor(f"bmr4_{b}", [128, TT], F32))
                sA = pc(nc.sbuf_tensor(f"sA4_{b}", [128, TT], F32))
                sB = pc(nc.sbuf_tensor(f"sB4_{b}", [128, TT], F32))
                o_t = pc(nc.sbuf_tensor(f"ot_{b}", [128, NCH * TT], F32))
                x_t = pc(nc.sbuf_tensor(f"xt4_{b}", [128, NCH * TT], F32))
                outb = pc(nc.sbuf_tensor(f"outb_{b}", [128, 2 * TT], F32))
                ps_all = PSA
                ps_fc = ps_all[:, 0:2 * TT]
                ps_st4 = ps_all[0:1, 2 * TT:4 * TT]
                ps_b4 = ps_all[:, 4 * TT:5 * TT]

                p = PG
                quiesce(p)
                fcmm = {}
                gel = {}
                sqv = {}
                st2 = {}
                pjmm = {}
                split_v = {}
                finev = {}
                out_tok = {}
                stats_free4 = None
                xn_toks = {}
                rec_tok = {}

                def load_xn(ti):
                    xb = (ti % 2) * NCH * TT
                    if ti >= 2:
                        p.wait("sync", cT, fcmm[(ti - 2) * NHC + NHC - 1])
                        p.wait("sync", cV, rec_tok[ti - 2])
                    toks = []
                    for ci in range(NCH):
                        toks.append(p.dma("sync", PIN, lambda s, ci=ci, ti=ti, xb=xb: s.dma_start(
                            out=xnh[:, xb + ci * TT:xb + (ci + 1) * TT], in_=xnh_sp[ti, ci])))
                        toks.append(p.dma("sync", PIN, lambda s, ci=ci, ti=ti, xb=xb: s.dma_start(
                            out=xnl[:, xb + ci * TT:xb + (ci + 1) * TT], in_=xnl_sp[ti, ci])))
                    xn_toks[ti] = toks

                load_xn(0)
                load_xn(1)
                for ti in range(NTT):
                    xb = (ti % 2) * NCH * TT
                    for tok in xn_toks[ti]:
                        p.wait("vector", tok[0], tok[1])
                        p.wait("tensor", tok[0], tok[1])
                    if ti > 0:
                        p.wait("vector", cT, pjmm[(ti - 1, NHC - 1)])
                        p.wait("tensor", cV, finev[(ti - 1) * NCH + 1])
                    for ci in range(NCH):
                        sl = slice(xb + ci * TT, xb + (ci + 1) * TT)
                        rec = p.op("vector", lambda v, sl=sl, ci=ci: v.tensor_add(
                            out=xnt[:, ci * TT:(ci + 1) * TT], in0=xnh[:, sl], in1=xnl[:, sl]), ch=cV)
                    rec_tok[ti] = rec

                    for h in range(NHC):
                        gi = ti * NHC + h
                        wb = (h % 2) * NCH * 128
                        if gi >= 2:
                            p.wait("sync", cT, fcmm[gi - 2])
                        wtok = p.dma("sync", PIN, lambda s, h=h, wb=wb: s.dma_start(
                            out=wfct[:, wb:wb + NCH * 128], in_=wfcq_sp[h]))
                        pb = slice((h % 2) * TT, (h % 2) * TT + TT)
                        if gi >= 2:
                            p.wait("tensor", cA, gel[gi - 2])
                        p.wait("tensor", wtok[0], wtok[1])
                        for ci in range(NCH):
                            wsl = slice(wb + ci * 128, wb + ci * 128 + 128)
                            p.op("tensor", lambda t, pb=pb, wsl=wsl, ci=ci, xb=xb: t.matmul(
                                ps_fc[:, pb], wfct[:, wsl], xnh[:, xb + ci * TT:xb + (ci + 1) * TT], start=(ci == 0), stop=False))
                            fcmm[gi] = p.op("tensor", lambda t, pb=pb, wsl=wsl, ci=ci, xb=xb: t.matmul(
                                ps_fc[:, pb], wfct[:, wsl], xnl[:, xb + ci * TT:xb + (ci + 1) * TT], start=False,
                                stop=(ci == NCH - 1)), ch=(cT if ci == NCH - 1 else None))
                        if h >= 1:
                            gprev = gi - 1
                            hp = h - 1
                            if hp == 0 and stats_free4 is not None:
                                p.wait("tensor", cV, stats_free4)
                            if hp == 0 and ti > 0:
                                p.wait("tensor", cV, finev[(ti - 1) * NCH + 3])
                            p.wait("tensor", cA, gel[gprev])
                            p.op("tensor", lambda t, hp=hp: t.matmul(
                                ps_st4[0:1, 0:TT], onesc[:, 0:1], g_sb[:, hp * TT:(hp + 1) * TT], start=(hp == 0), stop=(hp == NHC - 1)))
                            p.wait("tensor", cA, sqv[gprev])
                            st2[gprev] = p.op("tensor", lambda t, gprev=gprev: t.matmul(
                                ps_st4[0:1, TT:2 * TT], onesc[:, 0:1], g2s[:, (gprev % 2) * TT:(gprev % 2) * TT + TT],
                                start=(gprev % NHC == 0), stop=(gprev % NHC == NHC - 1)), ch=cT)
                        p.wait("scalar", cT, fcmm[gi])
                        gsl = slice(h * TT, (h + 1) * TT)
                        gel[gi] = p.op("scalar", lambda s, gsl=gsl, pb=pb: s.activation(
                            out=g_sb[:, gsl], in_=ps_fc[:, pb], func=AF.Gelu), ch=cA)
                        g2b = slice((gi % 2) * TT, (gi % 2) * TT + TT)
                        if gi >= 2:
                            p.wait("scalar", cT, st2[gi - 2])
                        sqv[gi] = p.op("scalar", lambda s, gsl=gsl, g2b=g2b: s.activation(
                            out=g2s[:, g2b], in_=g_sb[:, gsl], func=AF.Square), ch=cA)
                    if ti + 2 < NTT:
                        load_xn(ti + 2)
                    gi = ti * NHC + NHC - 1
                    p.wait("tensor", cA, gel[gi])
                    p.op("tensor", lambda t: t.matmul(
                        ps_st4[0:1, 0:TT], onesc[:, 0:1], g_sb[:, (NHC - 1) * TT:NHC * TT], start=False, stop=True))
                    p.wait("tensor", cA, sqv[gi])
                    st2[gi] = p.op("tensor", lambda t, gi=gi: t.matmul(
                        ps_st4[0:1, TT:2 * TT], onesc[:, 0:1], g2s[:, (gi % 2) * TT:(gi % 2) * TT + TT],
                        start=False, stop=True), ch=cT)
                    p.wait("vector", cT, st2[gi])
                    if ti > 0:
                        p.wait("tensor", cV, finev[(ti - 1) * NCH + NCH - 1])
                    stats_free4 = ln_bcast(p, ps_st4, ps_b4, HF, br_t, bmr_t, sA, sB, nr=False)
                    p.wait("vector", cV)

                    ftoks = {}
                    for ci in range(NCH):
                        obuf = slice(ci * TT, (ci + 1) * TT)
                        if ti > 0:
                            p.wait("sync", cV, finev[(ti - 1) * NCH + ci])
                        f1 = p.dma("sync", PIN, lambda s, ci=ci, ti=ti, obuf=obuf: s.dma_start(
                            out=o_t[:, obuf.start:obuf.start + 256], in_=o_sp[2 * ti, ci]))
                        f2 = p.dma("sync", PIN, lambda s, ci=ci, ti=ti, obuf=obuf: s.dma_start(
                            out=o_t[:, obuf.start + 256:obuf.start + 512], in_=o_sp[2 * ti + 1, ci]))
                        f3 = p.dma("sync", PIN, lambda s, ci=ci, ti=ti, obuf=obuf: s.dma_start(
                            out=x_t[:, obuf], in_=xin[b, ci * 128:(ci + 1) * 128, ti * TT:(ti + 1) * TT]))
                        ftoks[ci] = (f1, f2, f3)
                    ptoks = {}
                    for h in range(2):
                        if ti > 0:
                            p.wait("sync", cT, pjmm[(ti - 1, NHC - 2 + h)])
                        ptoks[h] = p.dma("sync", PIN, lambda s, h=h: s.dma_start(
                            out=wpj_t[:, (h % 2) * C:(h % 2) * C + C], in_=wprojq_sp[h]))
                    for h in range(NHC):
                        hb = slice((h % 2) * TT, (h % 2) * TT + TT)
                        wpb = (h % 2) * C
                        if h >= 2:
                            p.wait("sync", cT, pjmm[(ti, h - 2)])
                            ptoks[h] = p.dma("sync", PIN, lambda s, h=h, wpb=wpb: s.dma_start(
                                out=wpj_t[:, wpb:wpb + C], in_=wprojq_sp[h]))
                        ptok = ptoks[h]
                        gsl = slice(h * TT, (h + 1) * TT)
                        if h >= 2:
                            p.wait("vector", cT, pjmm[(ti, h - 2)])
                        p.op("vector", lambda v, gsl=gsl, hb=hb: v.tensor_mul(out=zs[:, hb], in0=g_sb[:, gsl], in1=br_t[:]))
                        p.op("vector", lambda v, hb=hb: v.tensor_sub(out=zs[:, hb], in0=zs[:, hb], in1=bmr_t[:]))
                        p.op("vector", lambda v, h=h, hb=hb: v.tensor_scalar(
                            out=hn32[:, hb], in0=zs[:, hb], scalar1=lnw_t[:, h:h + 1],
                            scalar2=lnb_t[:, h:h + 1], op0=OP.mult, op1=OP.add))
                        p.op("vector", lambda v, hb=hb: v.tensor_copy(out=hnh[:, hb], in_=hn32[:, hb]))
                        split_v[h] = p.op("vector", lambda v, hb=hb: v.tensor_sub(out=hnl[:, hb], in0=hn32[:, hb], in1=hnh[:, hb]), ch=cV)
                        p.wait("tensor", cV, split_v[h])
                        p.wait("tensor", ptok[0], ptok[1])
                        for cc in range(NCH):
                            wsl = slice(wpb + cc * 128, wpb + cc * 128 + 128)
                            fsl = slice(cc * TT, (cc + 1) * TT)
                            p.op("tensor", lambda t, wsl=wsl, fsl=fsl, h=h, hb=hb: t.matmul(
                                ps_all[:, fsl], wpj_t[:, wsl], hnh[:, hb], start=(h == 0), stop=False))
                            pjmm[(ti, h)] = p.op("tensor", lambda t, wsl=wsl, fsl=fsl, h=h, cc=cc, hb=hb: t.matmul(
                                ps_all[:, fsl], wpj_t[:, wsl], hnl[:, hb], start=False, stop=(h == NHC - 1)),
                                ch=(cT if cc == NCH - 1 else None))

                    p.wait("vector", cT, pjmm[(ti, NHC - 1)])
                    for ci in range(NCH):
                        gfi = ti * NCH + ci
                        obuf = slice(ci * TT, (ci + 1) * TT)
                        osl = slice((ci % 2) * TT, (ci % 2) * TT + TT)
                        for tok in ftoks[ci]:
                            p.wait("vector", tok[0], tok[1])
                        fsl = slice(ci * TT, (ci + 1) * TT)
                        if gfi >= 2 and out_tok.get(gfi - 2) is not None:
                            tok = out_tok[gfi - 2]
                            p.wait("vector", tok[0], tok[1])
                        p.op("vector", lambda v, fsl=fsl, obuf=obuf, osl=osl: v.tensor_add(out=outb[:, osl], in0=ps_all[:, fsl], in1=o_t[:, obuf]))
                        p.op("vector", lambda v, obuf=obuf, osl=osl: v.tensor_add(out=outb[:, osl], in0=outb[:, osl], in1=x_t[:, obuf]))
                        finev[gfi] = p.op("vector", lambda v, osl=osl, ci=ci: v.tensor_add(
                            out=outb[:, osl], in0=outb[:, osl], in1=xnt[:, ci * TT:(ci + 1) * TT]), ch=cV)
                        p.wait("sync", cV, finev[gfi])
                        out_tok[gfi] = p.dma("sync", POUT, lambda s, ci=ci, ti=ti, osl=osl: s.dma_start(
                            out=outT[b, ci * 128:(ci + 1) * 128, ti * TT:(ti + 1) * TT], in_=outb[:, osl]))
    
        _ph4()
    PG.wait_dma("sync", POUT)
    PG.run()

    stack.close()
    return nc


_NC_CACHE = None


def kernel(**inputs):
    global _NC_CACHE
    x = np.asarray(inputs["L0_attn_input_sequence"])
    wq_a = np.ascontiguousarray(np.asarray(inputs["L0_attn_Wq"], dtype=np.float32))
    wk_a = np.ascontiguousarray(np.asarray(inputs["L0_attn_Wk"], dtype=np.float32))
    wv_a = np.ascontiguousarray(np.asarray(inputs["L0_attn_Wv"], dtype=np.float32))
    wfcT_a = np.ascontiguousarray(np.asarray(inputs["L0_ffn_W_fc"]).T)
    wprojT_a = np.ascontiguousarray(np.asarray(inputs["L0_ffn_W_proj"]).T)

    def row(v):
        return np.ascontiguousarray(np.asarray(v, dtype=np.float32).reshape(1, -1))

    def tab(v, nch):
        return np.ascontiguousarray(np.asarray(v, dtype=np.float32).reshape(nch, 128).T)

    common = {
        "wq": wq_a, "wk": wk_a, "wv": wv_a,
        "bq": row(inputs["L0_attn_bq"]), "bk": row(inputs["L0_attn_bk"]), "bv": row(inputs["L0_attn_bv"]),
        "gq": tab(inputs["L0_attn_gq"], NCH), "gk": tab(inputs["L0_attn_gk"], NCH),
        "betaq": tab(inputs["L0_attn_betaq"], NCH), "betak": tab(inputs["L0_attn_betak"], NCH),
        "gv": row(inputs["L0_attn_gv"]), "betav": row(inputs["L0_attn_betav"]),
        "wfcT": wfcT_a, "wprojT": wprojT_a,
        "lnw": tab(inputs["L0_ffn_ln_w"], NHC), "lnb": tab(inputs["L0_ffn_ln_b"], NHC),
        "ident": np.eye(128, dtype=np.float32),
        "onesc": np.ascontiguousarray(np.stack([np.ones(128), np.zeros(128)], axis=1).astype(np.float32)),
        "onesc2": np.ascontiguousarray(np.stack([np.zeros(128), np.ones(128)], axis=1).astype(np.float32)),
        "onesrow": np.ones((1, TT), dtype=np.float32),
    }

    if _NC_CACHE is None:
        _NC_CACHE = build_nc()
    nc = _NC_CACHE

    in_maps = []
    for c in range(NCORES):
        xT = np.ascontiguousarray(x[c * BPC:(c + 1) * BPC].transpose(0, 2, 1))
        m = dict(common)
        m["xT"] = xT
        in_maps.append(m)

    res = run_bass_kernel_spmd(nc, in_maps, core_ids=list(range(NCORES)))
    out = np.concatenate(
        [np.asarray(r["outT"]).transpose(0, 2, 1) for r in res.results], axis=0)
    return np.ascontiguousarray(out, dtype=np.float32)


if __name__ == "__main__":
    import importlib.util as _il
    spec = _il.spec_from_file_location("_ref", "/root/problem/reference.py")
    ref = _il.module_from_spec(spec)
    spec.loader.exec_module(ref)
    ins = {k: np.asarray(v) for k, v in ref.setup_inputs().items()}
    got = kernel(**ins)
    exp = np.asarray(ref.reference(**ins))
    err = np.linalg.norm(got - exp) / np.linalg.norm(exp)
    print("rel_err:", err)


# version 16 (final)
# speedup vs baseline: 1.0476x; 1.0476x over previous
import numpy as np
from contextlib import ExitStack

import concourse.bass as bass
import concourse.mybir as mybir
from concourse.bass_utils import run_bass_kernel_spmd

F32 = mybir.dt.float32
BF16 = mybir.dt.bfloat16
AF = mybir.ActivationFunctionType
OP = mybir.AluOpType
AX = mybir.AxisListType

B, L, C, HF = 32, 2048, 1024, 4096
NCORES = 8
BPC = B // NCORES
NCH = C // 128
NHC = HF // 128
TT = 512
NTT = L // TT
LEPS = 1e-5
GEPS = 1e-6


class Prog:

    def __init__(self, nc):
        self.nc = nc
        self.ops = {"sync": [], "tensor": [], "vector": [], "scalar": [], "gpsimd": []}

    def op(self, eng, fn, ch=None, k=1):
        if ch is not None:
            ch.n += k
            self.ops[eng].append(lambda e, fn=fn, ch=ch, k=k: fn(e).then_inc(ch.sem, k))
            return ch.n
        self.ops[eng].append(fn)
        return None

    def wait(self, eng, ch, val=None):
        v = ch.n if val is None else val
        assert v <= ch.n, (eng, v, ch.n)
        self.ops[eng].append(lambda e, s=ch.sem, v=v: e.wait_ge(s, v))

    def dma(self, eng, pool, fn):
        ch = pool.take()
        if ch.n > 0:
            self.wait(eng, ch)
        n = self.op(eng, fn, ch=ch, k=16)
        return (ch, n)

    def wait_dma(self, eng, pool):
        for ch in pool.chans:
            if ch.n > 0:
                self.wait(eng, ch)

    def run(self):
        with self.nc.Block() as blk:
            for name in ("sync", "tensor", "vector", "scalar", "gpsimd"):
                ops = self.ops[name]
                if ops:
                    getattr(blk, name)(lambda eng, ops=tuple(ops): [f(eng) for f in ops])


class Chan:
    def __init__(self, stack, nc, name):
        self.sem = stack.enter_context(nc.semaphore(name))
        self.n = 0


class Pool:
    def __init__(self, stack, nc, name, width):
        self.chans = [Chan(stack, nc, f"{name}{i}") for i in range(width)]
        self.i = 0

    def take(self):
        ch = self.chans[self.i % len(self.chans)]
        self.i += 1
        return ch


def build_nc(dbg=False):
    nc = bass.Bass("TRN2", target_bir_lowering=False, detect_race_conditions=False)

    xin = nc.dram_tensor("xT", [BPC, C, L], F32, kind="ExternalInput").ap()
    wq = nc.dram_tensor("wq", [C, C], F32, kind="ExternalInput").ap()
    wk = nc.dram_tensor("wk", [C, C], F32, kind="ExternalInput").ap()
    wv = nc.dram_tensor("wv", [C, C], F32, kind="ExternalInput").ap()
    bq_d = nc.dram_tensor("bq", [1, C], F32, kind="ExternalInput").ap()
    bk_d = nc.dram_tensor("bk", [1, C], F32, kind="ExternalInput").ap()
    bv_d = nc.dram_tensor("bv", [1, C], F32, kind="ExternalInput").ap()
    gq_d = nc.dram_tensor("gq", [128, NCH], F32, kind="ExternalInput").ap()
    gk_d = nc.dram_tensor("gk", [128, NCH], F32, kind="ExternalInput").ap()
    bbq_d = nc.dram_tensor("betaq", [128, NCH], F32, kind="ExternalInput").ap()
    bbk_d = nc.dram_tensor("betak", [128, NCH], F32, kind="ExternalInput").ap()
    gv_d = nc.dram_tensor("gv", [1, C], F32, kind="ExternalInput").ap()
    bbv_d = nc.dram_tensor("betav", [1, C], F32, kind="ExternalInput").ap()
    wfcT = nc.dram_tensor("wfcT", [C, HF], F32, kind="ExternalInput").ap()
    wprojT = nc.dram_tensor("wprojT", [HF, C], F32, kind="ExternalInput").ap()
    lnw_d = nc.dram_tensor("lnw", [128, NHC], F32, kind="ExternalInput").ap()
    lnb_d = nc.dram_tensor("lnb", [128, NHC], F32, kind="ExternalInput").ap()
    ident_d = nc.dram_tensor("ident", [128, 128], F32, kind="ExternalInput").ap()
    onesc_d = nc.dram_tensor("onesc", [128, 2], F32, kind="ExternalInput").ap()
    onesc2_d = nc.dram_tensor("onesc2", [128, 2], F32, kind="ExternalInput").ap()
    onesrow_d = nc.dram_tensor("onesrow", [1, TT], F32, kind="ExternalInput").ap()

    outT = nc.dram_tensor("outT", [BPC, C, L], F32, kind="ExternalOutput").ap()

    _k = "ExternalOutput" if dbg else "Internal"
    xnh_sp = nc.dram_tensor("xnh_sp", [NTT, NCH, 128, TT], BF16, kind=_k).ap()
    xnl_sp = nc.dram_tensor("xnl_sp", [NTT, NCH, 128, TT], BF16, kind=_k).ap()
    qn_sp = nc.dram_tensor("qn_sp", [NTT, NCH, 128, 2 * TT], BF16, kind=_k).ap()
    kn_sp = nc.dram_tensor("kn_sp", [NTT, NCH, 128, 2 * TT], BF16, kind=_k).ap()
    o_sp = nc.dram_tensor("o_sp", [2 * NTT, NCH, 128, 256], F32, kind=_k).ap()
    vn_sp = nc.dram_tensor("vn_sp", [16, 128, 2 * C], BF16, kind=_k).ap()
    wqkv_sp = nc.dram_tensor("wqkv_sp", [3, NCH, 128, 2 * C], BF16, kind=_k).ap()
    wfcq_sp = nc.dram_tensor("wfcq_sp", [NHC, 128, NCH * 128], BF16, kind=_k).ap()
    wprojq_sp = nc.dram_tensor("wprojq_sp", [NHC, 128, C], BF16, kind=_k).ap()
    gvb_sp = nc.dram_tensor("gvb_sp", [2, 128, C], F32, kind=_k).ap()

    stack = ExitStack()
    ctx = stack.enter_context

    PIN = Pool(stack, nc, "cIN", 6)
    PSP = Pool(stack, nc, "cSP", 8)
    POUT = Pool(stack, nc, "cOUT", 2)
    cT = Chan(stack, nc, "cT")
    cV = Chan(stack, nc, "cV")
    cA = Chan(stack, nc, "cA")

    ident = ctx(nc.sbuf_tensor("ident_s", [128, 128], F32))
    onesc = ctx(nc.sbuf_tensor("onesc_s", [128, 2], F32))
    onesc2 = ctx(nc.sbuf_tensor("onesc2_s", [128, 2], F32))
    onesrow = ctx(nc.sbuf_tensor("onesrow_s", [1, TT], F32))
    gq_t = ctx(nc.sbuf_tensor("gq_t", [128, NCH], F32))
    gk_t = ctx(nc.sbuf_tensor("gk_t", [128, NCH], F32))
    bbq_t = ctx(nc.sbuf_tensor("bbq_t", [128, NCH], F32))
    bbk_t = ctx(nc.sbuf_tensor("bbk_t", [128, NCH], F32))
    lnw_t = ctx(nc.sbuf_tensor("lnw_t", [128, NHC], F32))
    lnb_t = ctx(nc.sbuf_tensor("lnb_t", [128, NHC], F32))
    st_row = ctx(nc.sbuf_tensor("st_row", [1, 2 * TT], F32))
    thr_p = ctx(nc.sbuf_tensor("thr_p", [128, 1], F32))
    nthr_p = ctx(nc.sbuf_tensor("nthr_p", [128, 1], F32))
    grow = ctx(nc.sbuf_tensor("grow", [1, 1], F32))
    PSA = ctx(nc.psum_tensor("psarena", [128, NCH * TT], F32))

    def quiesce(p, pools=("PIN", "PSP", "POUT")):
        pm = {"PIN": PIN, "PSP": PSP, "POUT": POUT}
        for eng in ("sync", "gpsimd", "vector", "scalar", "tensor"):
            for ch in (cT, cV, cA):
                if ch.n > 0:
                    p.wait(eng, ch)
            for pn in pools:
                p.wait_dma(eng, pm[pn])

    def vseq(p, fns):
        for fn in fns:
            p.wait("vector", cV)
            p.op("vector", fn, ch=cV)

    def ln_bcast(p, ps_st, ps_b, n_feat, br_t, bmr_t, sA, sB, nr=True):
        stats_free = p.op("vector", lambda v: v.tensor_copy(out=st_row[:], in_=ps_st[:]), ch=cV)
        p.wait("tensor", cV)
        p.op("tensor", lambda t: t.matmul(ps_b[:], onesrow[0:1, 0:128], st_row[0:1, 0:TT], start=True, stop=True), ch=cT)
        p.wait("vector", cT)
        p.op("vector", lambda v: v.tensor_scalar_mul(out=bmr_t[:], in0=ps_b[:], scalar1=1.0 / n_feat), ch=cV)
        p.wait("tensor", cV)
        p.op("tensor", lambda t: t.matmul(ps_b[:], onesrow[0:1, 0:128], st_row[0:1, TT:2 * TT], start=True, stop=True), ch=cT)
        p.wait("vector", cT)
        p.op("vector", lambda v: v.tensor_scalar_mul(out=sA[:], in0=ps_b[:], scalar1=1.0 / n_feat))
        p.op("vector", lambda v: v.tensor_mul(out=sB[:], in0=bmr_t[:], in1=bmr_t[:]))
        p.op("vector", lambda v: v.tensor_sub(out=sA[:], in0=sA[:], in1=sB[:]))
        p.op("vector", lambda v: v.tensor_scalar_add(out=sA[:], in0=sA[:], scalar1=LEPS), ch=cV)
        p.wait("scalar", cV)
        p.op("scalar", lambda s: s.activation(out=sB[:], in_=sA[:], func=AF.Sqrt), ch=cA)
        p.wait("vector", cA)
        if nr:
            p.op("vector", lambda v: v.reciprocal(out=sB[:], in_=sB[:]))
            p.op("vector", lambda v: v.tensor_mul(out=br_t[:], in0=sB[:], in1=sB[:]))
            p.op("vector", lambda v: v.tensor_mul(out=br_t[:], in0=br_t[:], in1=sA[:]))
            p.op("vector", lambda v: v.tensor_scalar(out=br_t[:], in0=br_t[:], scalar1=-0.5, scalar2=1.5, op0=OP.mult, op1=OP.add))
            p.op("vector", lambda v: v.tensor_mul(out=br_t[:], in0=sB[:], in1=br_t[:]))
        else:
            p.op("vector", lambda v: v.reciprocal(out=br_t[:], in_=sB[:]))
        p.op("vector", lambda v: v.tensor_mul(out=bmr_t[:], in0=bmr_t[:], in1=br_t[:]), ch=cV)
        return stats_free

    PG = Prog(nc)

    def _p0():
        with ExitStack() as ph:
            pc = ph.enter_context
            gv_r = pc(nc.sbuf_tensor("gv_r", [1, C], F32))
            bbv_r = pc(nc.sbuf_tensor("bbv_r", [1, C], F32))
            bgv = pc(nc.sbuf_tensor("bgv0", [128, C], F32))
            bbvB = pc(nc.sbuf_tensor("bbvB0", [128, C], F32))
            wld = pc(nc.sbuf_tensor("p0_wld", [128, 2 * HF], F32))
            am = pc(nc.sbuf_tensor("p0_am", [128, 2 * HF], BF16))
            bm = pc(nc.sbuf_tensor("p0_bm", [128, 2 * HF], BF16))
            wqs = pc(nc.sbuf_tensor("p0_wqs", [128, 2 * HF], BF16))
            acol = pc(nc.sbuf_tensor("p0_acol", [128, 1], F32))
            asum = pc(nc.sbuf_tensor("p0_asum", [128, 1], F32))
            ps_g = PSA[0:1, 7 * TT:7 * TT + 1]
            ps_b = PSA[:, 0:C]

            p = PG
            for src, dst in ((ident_d, ident), (onesc_d, onesc), (onesc2_d, onesc2),
                             (onesrow_d, onesrow), (gq_d, gq_t), (gk_d, gk_t),
                             (bbq_d, bbq_t), (bbk_d, bbk_t), (lnw_d, lnw_t), (lnb_d, lnb_t),
                             (gv_d, gv_r), (bbv_d, bbv_r)):
                p.dma("sync", PIN, lambda s, a=src, b=dst: s.dma_start(out=b[:], in_=a[:]))

            p.wait_dma("tensor", PIN)
            p.op("tensor", lambda t: t.matmul(ps_b[:, 0:TT], onesrow[0:1, 0:128], gv_r[0:1, 0:TT], start=True, stop=True))
            p.op("tensor", lambda t: t.matmul(ps_b[:, TT:C], onesrow[0:1, 0:128], gv_r[0:1, TT:C], start=True, stop=True), ch=cT)
            p.wait("vector", cT)
            p.op("vector", lambda v: v.tensor_copy(out=bgv[:], in_=ps_b[:]), ch=cV)
            p.wait("tensor", cV)
            p.op("tensor", lambda t: t.matmul(ps_b[:, 0:TT], onesrow[0:1, 0:128], bbv_r[0:1, 0:TT], start=True, stop=True))
            p.op("tensor", lambda t: t.matmul(ps_b[:, TT:C], onesrow[0:1, 0:128], bbv_r[0:1, TT:C], start=True, stop=True), ch=cT)
            p.wait("vector", cT)
            p.op("vector", lambda v: v.tensor_copy(out=bbvB[:], in_=ps_b[:]), ch=cV)
            p.wait("gpsimd", cV)
            p.dma("gpsimd", PSP, lambda g, bgv=bgv: g.dma_start(out=gvb_sp[0], in_=bgv[:]))
            p.dma("gpsimd", PSP, lambda g, bbvB=bbvB: g.dma_start(out=gvb_sp[1], in_=bbvB[:]))

            acols = pc(nc.sbuf_tensor("p0_acols", [128, NCH + NHC], F32))
            thr2_p = pc(nc.sbuf_tensor("p0_thr2", [128, 1], F32))
            nthr2_p = pc(nc.sbuf_tensor("p0_nthr2", [128, 1], F32))
            MATS = ((wfcT, NCH, HF, "fc", 0), (wprojT, NHC, C, "pj", NCH))
            red = {}
            gq = 0
            for (wsrc, nchunks, fchunk, tag, abase) in MATS:
                for ci in range(nchunks):
                    slot = (gq % 2) * HF
                    if gq >= 2:
                        p.wait("sync", cV, red[gq - 2])
                    t1 = p.dma("sync", PIN, lambda s, ci=ci, wsrc=wsrc, f=fchunk, slot=slot: s.dma_start(
                        out=wld[:, slot:slot + f // 2], in_=wsrc[ci * 128:(ci + 1) * 128, 0:f // 2]))
                    t2 = p.dma("sync", PIN, lambda s, ci=ci, wsrc=wsrc, f=fchunk, slot=slot: s.dma_start(
                        out=wld[:, slot + f // 2:slot + f], in_=wsrc[ci * 128:(ci + 1) * 128, f // 2:f]))
                    p.wait("vector", t1[0], t1[1])
                    p.wait("vector", t2[0], t2[1])
                    red[gq] = p.op("vector", lambda v, f=fchunk, slot=slot, ac=abase + ci: v.reduce_sum(
                        out=acols[:, ac:ac + 1], in_=wld[:, slot:slot + f], axis=AX.X,
                        apply_absolute_value=True), ch=cV)
                    gq += 1
            for (wsrc, nchunks, fchunk, tag, abase) in MATS:
                tp_, np_ = (thr_p, nthr_p) if tag == "fc" else (thr2_p, nthr2_p)
                p.wait("vector", cV)
                vseq(p, [lambda v, abase=abase, nchunks=nchunks: v.reduce_sum(
                    out=acol[:], in_=acols[:, abase:abase + nchunks], axis=AX.X)])
                p.wait("tensor", cV)
                p.op("tensor", lambda t: t.matmul(ps_g[:], onesc[:, 0:1], acol[:], start=True, stop=True), ch=cT)
                p.wait("vector", cT)
                vseq(p, [lambda v: v.tensor_scalar_mul(out=grow[:], in0=ps_g[:], scalar1=1.0 / (C * HF)),
                         lambda v: v.tensor_scalar_add(out=grow[:], in0=grow[:], scalar1=GEPS)])
                p.wait("tensor", cV)
                p.op("tensor", lambda t: t.matmul(ps_b[:, 0:1], onesrow[0:1, 0:128], grow[:], start=True, stop=True), ch=cT)
                p.wait("vector", cT)
                vseq(p, [lambda v, tp_=tp_: v.tensor_scalar_mul(out=tp_[:], in0=ps_b[:, 0:1], scalar1=0.5),
                         lambda v, tp_=tp_, np_=np_: v.tensor_scalar_mul(out=np_[:], in0=tp_[:], scalar1=-1.0)])
            qvv = {}
            sptoks = {}
            gq = 0
            for (wsrc, nchunks, fchunk, tag, abase) in MATS:
                tp_, np_ = (thr_p, nthr_p) if tag == "fc" else (thr2_p, nthr2_p)
                for ci in range(nchunks):
                    slot = (gq % 2) * HF
                    if gq < 2:
                        p.wait("sync", cV, red[38 + gq])
                    else:
                        p.wait("sync", cV, qvv[gq - 2])
                    t1 = p.dma("sync", PIN, lambda s, ci=ci, wsrc=wsrc, f=fchunk, slot=slot: s.dma_start(
                        out=wld[:, slot:slot + f // 2], in_=wsrc[ci * 128:(ci + 1) * 128, 0:f // 2]))
                    t2 = p.dma("sync", PIN, lambda s, ci=ci, wsrc=wsrc, f=fchunk, slot=slot: s.dma_start(
                        out=wld[:, slot + f // 2:slot + f], in_=wsrc[ci * 128:(ci + 1) * 128, f // 2:f]))
                    p.wait("vector", t1[0], t1[1])
                    p.wait("vector", t2[0], t2[1])
                    if gq >= 2:
                        for tok in sptoks[gq - 2]:
                            p.wait("vector", tok[0], tok[1])
                    p.op("vector", lambda v, f=fchunk, slot=slot, tp_=tp_: v.tensor_single_scalar(
                        out=am[:, slot:slot + f], in_=wld[:, slot:slot + f], scalar=tp_[:], op=OP.is_gt))
                    p.op("vector", lambda v, f=fchunk, slot=slot, np_=np_: v.tensor_single_scalar(
                        out=bm[:, slot:slot + f], in_=wld[:, slot:slot + f], scalar=np_[:], op=OP.is_lt))
                    qvv[gq] = p.op("vector", lambda v, f=fchunk, slot=slot: v.tensor_sub(
                        out=wqs[:, slot:slot + f], in0=am[:, slot:slot + f], in1=bm[:, slot:slot + f]), ch=cV)
                    p.wait("gpsimd", cV, qvv[gq])
                    toks = []
                    if tag == "fc":
                        for h in range(NHC):
                            toks.append(p.dma("gpsimd", PSP, lambda g, ci=ci, h=h, slot=slot: g.dma_start(
                                out=wfcq_sp[h][:, ci * 128:(ci + 1) * 128], in_=wqs[:, slot + h * 128:slot + (h + 1) * 128])))
                    else:
                        toks.append(p.dma("gpsimd", PSP, lambda g, ci=ci, slot=slot: g.dma_start(
                            out=wprojq_sp[ci], in_=wqs[:, slot:slot + C])))
                    sptoks[gq] = toks
                    gq += 1
            wsp = pc(nc.sbuf_tensor("p0_wsp", [128, 2 * 2 * C], BF16))
            wtoks = {}
            wsptoks = {}
            for wi, wsrc in enumerate((wq, wk, wv)):
                for ci in range(NCH):
                    gwi = wi * NCH + ci
                    half = slice((gwi % 2) * C, (gwi % 2) * C + C)
                    if gwi >= 2:
                        p.wait("sync", cV, wtoks[gwi - 2])
                    else:
                        p.wait("sync", cV)
                        p.wait("sync", cA)
                    ltok = p.dma("sync", PIN, lambda s, ci=ci, wsrc=wsrc, half=half: s.dma_start(
                        out=wld[:, half], in_=wsrc[ci * 128:(ci + 1) * 128, :]))
                    p.wait("vector", ltok[0], ltok[1])
                    wofs = (gwi % 2) * 2 * C
                    if gwi >= 2 and wsptoks.get(gwi - 2) is not None:
                        tok = wsptoks[gwi - 2]
                        p.wait("vector", tok[0], tok[1])
                    p.op("vector", lambda v, half=half, wofs=wofs: v.tensor_copy(
                        out=wsp[:, wofs:wofs + C], in_=wld[:, half]))
                    wtoks[gwi] = p.op("vector", lambda v, half=half, wofs=wofs: v.tensor_sub(
                        out=wsp[:, wofs + C:wofs + 2 * C], in0=wld[:, half], in1=wsp[:, wofs:wofs + C]), ch=cV)
                    p.wait("gpsimd", cV, wtoks[gwi])
                    wsptoks[gwi] = p.dma("gpsimd", PSP, lambda g, wi=wi, ci=ci, wofs=wofs: g.dma_start(
                        out=wqkv_sp[wi, ci], in_=wsp[:, wofs:wofs + 2 * C]))

    _p0()
    for b in range(BPC):
        s_xn = ExitStack()
        xnh_sb = s_xn.enter_context(nc.sbuf_tensor(f"xnh_{b}", [128, NCH * L], BF16))
        xnl_sb = s_xn.enter_context(nc.sbuf_tensor(f"xnl_{b}", [128, NCH * L], BF16))
        w_sb = s_xn.enter_context(nc.sbuf_tensor(f"w_{b}", [128, 2 * NCH * 2 * C], BF16))
        wtok_h = {}

        def load_w(p, wi, wbuf, gate=None, w_sb=w_sb):
            if gate is not None:
                p.wait("sync", cT, gate)
            toks = []
            for ci in range(NCH):
                toks.append(p.dma("sync", PIN, lambda s, ci=ci, wi=wi, wbuf=wbuf, w_sb=w_sb: s.dma_start(
                    out=w_sb[:, wbuf * NCH * 2 * C + ci * 2 * C: wbuf * NCH * 2 * C + (ci + 1) * 2 * C],
                    in_=wqkv_sp[wi, ci])))
            return toks

        def _ph1(b=b, xnh_sb=xnh_sb, xnl_sb=xnl_sb, load_w=load_w, wtok_h=wtok_h):
            with ExitStack() as ph:
                pc = ph.enter_context
                xt_sb = pc(nc.sbuf_tensor(f"xt_{b}", [128, 2 * NCH * TT], F32))
                x2s = pc(nc.sbuf_tensor(f"x2s_{b}", [128, 2 * TT], F32))
                xw = pc(nc.sbuf_tensor(f"xw_{b}", [128, TT], F32))
                br_t = pc(nc.sbuf_tensor(f"br1_{b}", [128, TT], F32))
                bmr_t = pc(nc.sbuf_tensor(f"bmr1_{b}", [128, TT], F32))
                sA = pc(nc.sbuf_tensor(f"sA1_{b}", [128, TT], F32))
                sB = pc(nc.sbuf_tensor(f"sB1_{b}", [128, TT], F32))
                ps_st = PSA[0:1, 0:2 * TT]
                ps_b1 = PSA[:, 2 * TT:3 * TT]

                p = PG
                quiesce(p)
                xtoks = {}

                def load_xt(ti):
                    xb = (ti % 2) * NCH * TT
                    for ci in range(NCH):
                        xtoks[(ti, ci)] = p.dma("sync", PIN, lambda s, ci=ci, ti=ti, xb=xb: s.dma_start(
                            out=xt_sb[:, xb + ci * TT: xb + (ci + 1) * TT],
                            in_=xin[b, ci * 128:(ci + 1) * 128, ti * TT:(ti + 1) * TT]))

                load_xt(0)
                load_xt(1)
                wtok_h["q"] = load_w(p, 0, 0)
                wtok_h["k"] = load_w(p, 1, 1)
                stats_free = None
                sqv = {}
                st2 = {}
                xuse = {}
                for ti in range(NTT):
                    if stats_free is not None:
                        p.wait("tensor", cV, stats_free)
                    for ci in range(NCH):
                        gi = ti * NCH + ci
                        xb = (ti % 2) * NCH * TT
                        sl = slice(xb + ci * TT, xb + (ci + 1) * TT)
                        nsl = slice(ci * L + ti * TT, ci * L + ti * TT + TT)
                        buf = slice((ci % 2) * TT, (ci % 2) * TT + TT)
                        tok = xtoks[(ti, ci)]
                        p.wait("scalar", tok[0], tok[1])
                        p.wait("tensor", tok[0], tok[1])
                        if gi >= 2:
                            p.wait("scalar", cT, st2[gi - 2])
                        sqv[gi] = p.op("scalar", lambda s, sl=sl, buf=buf: s.activation(
                            out=x2s[:, buf], in_=xt_sb[:, sl], func=AF.Square), ch=cA)
                        p.op("tensor", lambda t, sl=sl, ci=ci: t.matmul(
                            ps_st[0:1, 0:TT], onesc[:, 0:1], xt_sb[:, sl], start=(ci == 0), stop=(ci == NCH - 1)))
                        p.wait("tensor", cA, sqv[gi])
                        st2[gi] = p.op("tensor", lambda t, buf=buf, ci=ci: t.matmul(
                            ps_st[0:1, TT:2 * TT], onesc[:, 0:1], x2s[:, buf], start=(ci == 0), stop=(ci == NCH - 1)), ch=cT)
                    p.wait("vector", cT, st2[ti * NCH + NCH - 1])
                    stats_free = ln_bcast(p, ps_st, ps_b1, C, br_t, bmr_t, sA, sB)
                    p.wait("vector", cV)
                    for ci in range(NCH):
                        xb = (ti % 2) * NCH * TT
                        sl = slice(xb + ci * TT, xb + (ci + 1) * TT)
                        nsl = slice(ci * L + ti * TT, ci * L + ti * TT + TT)
                        p.op("vector", lambda v, sl=sl: v.tensor_mul(out=xw[:], in0=xt_sb[:, sl], in1=br_t[:]))
                        p.op("vector", lambda v: v.tensor_sub(out=xw[:], in0=xw[:], in1=bmr_t[:]))
                        p.op("vector", lambda v, nsl=nsl: v.tensor_copy(out=xnh_sb[:, nsl], in_=xw[:]))
                        xv = p.op("vector", lambda v, nsl=nsl: v.tensor_sub(out=xnl_sb[:, nsl], in0=xw[:], in1=xnh_sb[:, nsl]), ch=cV)
                        p.wait("gpsimd", cV, xv)
                        p.dma("gpsimd", PSP, lambda g, ci=ci, ti=ti, nsl=nsl: g.dma_start(
                            out=xnh_sp[ti, ci], in_=xnh_sb[:, nsl]))
                        p.dma("gpsimd", PSP, lambda g, ci=ci, ti=ti, nsl=nsl: g.dma_start(
                            out=xnl_sp[ti, ci], in_=xnl_sb[:, nsl]))
                    xuse[ti] = (cT.n, cV.n, cA.n)
                    if ti + 2 < NTT:
                        p.wait("sync", cT, xuse[ti][0])
                        p.wait("sync", cV, xuse[ti][1])
                        p.wait("sync", cA, xuse[ti][2])
                        load_xt(ti + 2)
    
        _ph1()
        def _ph2(b=b, xnh_sb=xnh_sb, xnl_sb=xnl_sb, w_sb=w_sb, load_w=load_w, wtok_h=wtok_h):
            with ExitStack() as ph:
                pc = ph.enter_context
                y_tile = pc(nc.sbuf_tensor(f"yt_{b}", [128, NCH * TT], F32))
                y2s = pc(nc.sbuf_tensor(f"y2s_{b}", [128, 2 * TT], F32))
                br_t = pc(nc.sbuf_tensor(f"br2_{b}", [128, TT], F32))
                bmr_t = pc(nc.sbuf_tensor(f"bmr2_{b}", [128, TT], F32))
                sA = pc(nc.sbuf_tensor(f"sA2_{b}", [128, TT], F32))
                sB = pc(nc.sbuf_tensor(f"sB2_{b}", [128, TT], F32))
                qn_out = pc(nc.sbuf_tensor(f"qno_{b}", [128, TT], F32))
                qs_out = pc(nc.sbuf_tensor(f"qso_{b}", [128, 5 * 2 * TT], BF16))
                c_mu = pc(nc.sbuf_tensor(f"cmu_{b}", [128, 1], F32))
                c_msq = pc(nc.sbuf_tensor(f"cmsq_{b}", [128, 1], F32))
                c_t0 = pc(nc.sbuf_tensor(f"ct0_{b}", [128, 1], F32))
                c_a = pc(nc.sbuf_tensor(f"ca_{b}", [128, 1], F32))
                c_r = pc(nc.sbuf_tensor(f"cr_{b}", [128, 1], F32))
                s1c = pc(nc.sbuf_tensor(f"s1c_{b}", [128, 1], F32))
                s2c = pc(nc.sbuf_tensor(f"s2c_{b}", [128, 1], F32))
                ps_y = PSA[:, 0:2 * TT]
                ps_st = PSA[0:1, 2 * TT:4 * TT]
                ps_b1 = PSA[:, 4 * TT:5 * TT]
                ps_v = PSA[:, 5 * TT:7 * TT]
                vnob = pc(nc.sbuf_tensor(f"vno_{b}", [128, C], F32))
                vns = pc(nc.sbuf_tensor(f"vns_{b}", [128, 2 * 2 * C], BF16))
                bq_r = pc(nc.sbuf_tensor(f"bq_{b}", [1, C], F32))
                bk_r = pc(nc.sbuf_tensor(f"bk_{b}", [1, C], F32))
                bv_r = pc(nc.sbuf_tensor(f"bv_{b}", [1, C], F32))
                bgv = pc(nc.sbuf_tensor(f"bgv_{b}", [128, C], F32))
                bbvB = pc(nc.sbuf_tensor(f"bbvB_{b}", [128, C], F32))
                v_sbs = (y_tile[:, 0:C], y_tile[:, C:2 * C])
                v2ss = (y_tile[:, 2 * C:3 * C], y_tile[:, 3 * C:4 * C])

                p = PG
                quiesce(p, pools=("PIN",))
                for src_, dst_ in ((bq_d, bq_r), (bk_d, bk_r), (bv_d, bv_r)):
                    btok = p.dma("sync", PIN, lambda s, src_=src_, dst_=dst_: s.dma_start(out=dst_[:], in_=src_[:]))
                    p.wait("tensor", btok[0], btok[1])
                for gi_, dst_ in ((0, bgv), (1, bbvB)):
                    btok = p.dma("sync", PIN, lambda s, gi_=gi_, dst_=dst_: s.dma_start(out=dst_[:], in_=gvb_sp[gi_]))
                    p.wait("vector", btok[0], btok[1])
                last_w_use = [None]
                evac = {}
                sqv = {}
                st2 = {}
                spill = {}
                apply_v = {}
                gbase = [0]

                def proj_ln(p, wbuf, wtoks, b_row, g_tab, bb_tab, dst_sp):
                    for tok in wtoks:
                        p.wait("tensor", tok[0], tok[1])
                    wofs = wbuf * NCH * 2 * C
                    stats_free = None
                    g0 = gbase[0]
                    gbase[0] += NTT * NCH
                    for ti in range(NTT):
                        if stats_free is not None:
                            p.wait("tensor", cV, stats_free)
                        for h in range(NCH):
                            gi = g0 + ti * NCH + h
                            pb = slice((h % 2) * TT, (h % 2) * TT + TT)
                            if gi >= 2:
                                p.wait("tensor", cA, evac[gi - 2])
                            for ci in range(NCH):
                                xh = slice(ci * L + ti * TT, ci * L + ti * TT + TT)
                                whi = slice(wofs + ci * 2 * C + h * 128, wofs + ci * 2 * C + h * 128 + 128)
                                wlo = slice(wofs + ci * 2 * C + C + h * 128, wofs + ci * 2 * C + C + h * 128 + 128)
                                p.op("tensor", lambda t, pb=pb, whi=whi, xh=xh, ci=ci: t.matmul(
                                    ps_y[:, pb], w_sb[:, whi], xnh_sb[:, xh], start=(ci == 0), stop=False))
                                p.op("tensor", lambda t, pb=pb, whi=whi, xh=xh: t.matmul(
                                    ps_y[:, pb], w_sb[:, whi], xnl_sb[:, xh], start=False, stop=False))
                                p.op("tensor", lambda t, pb=pb, wlo=wlo, xh=xh: t.matmul(
                                    ps_y[:, pb], w_sb[:, wlo], xnh_sb[:, xh], start=False, stop=False))
                            bias_val = p.op("tensor", lambda t, pb=pb, h=h: t.matmul(
                                ps_y[:, pb], b_row[0:1, h * 128:(h + 1) * 128], onesrow[:], start=False, stop=True), ch=cT)
                            last_w_use[0] = bias_val
                            p.wait("scalar", cT, bias_val)
                            if gi >= NCH:
                                p.wait("scalar", cV, apply_v[gi - NCH])
                            ysl = slice(h * TT, (h + 1) * TT)
                            evac[gi] = p.op("scalar", lambda s, ysl=ysl, pb=pb: s.copy(out=y_tile[:, ysl], in_=ps_y[:, pb]), ch=cA)
                            y2b = slice((gi % 2) * TT, (gi % 2) * TT + TT)
                            if gi >= 2:
                                p.wait("scalar", cT, st2[gi - 2])
                            sqv[gi] = p.op("scalar", lambda s, ysl=ysl, y2b=y2b: s.activation(
                                out=y2s[:, y2b], in_=y_tile[:, ysl], func=AF.Square), ch=cA)
                            p.wait("tensor", cA, evac[gi])
                            p.op("tensor", lambda t, ysl=ysl, h=h: t.matmul(
                                ps_st[0:1, 0:TT], onesc[:, 0:1], y_tile[:, ysl], start=(h == 0), stop=(h == NCH - 1)))
                            p.wait("tensor", cA, sqv[gi])
                            st2[gi] = p.op("tensor", lambda t, y2b=y2b, h=h: t.matmul(
                                ps_st[0:1, TT:2 * TT], onesc[:, 0:1], y2s[:, y2b], start=(h == 0), stop=(h == NCH - 1)), ch=cT)
                        p.wait("vector", cT, st2[g0 + ti * NCH + NCH - 1])
                        stats_free = ln_bcast(p, ps_st, ps_b1, C, br_t, bmr_t, sA, sB)
                        p.wait("vector", cV)
                        for h in range(NCH):
                            gi = g0 + ti * NCH + h
                            ysl = slice(h * TT, (h + 1) * TT)
                            so = gi % 5
                            shi = slice(so * 2 * TT, so * 2 * TT + TT)
                            slo = slice(so * 2 * TT + TT, so * 2 * TT + 2 * TT)
                            if gi >= 5 and spill.get(gi - 5) is not None:
                                tok = spill[gi - 5]
                                p.wait("vector", tok[0], tok[1])
                            apply_v[gi] = p.op("vector", lambda v, ysl=ysl: v.tensor_mul(out=qn_out[:], in0=y_tile[:, ysl], in1=br_t[:]), ch=cV)
                            p.op("vector", lambda v: v.tensor_sub(out=qn_out[:], in0=qn_out[:], in1=bmr_t[:]))
                            p.op("vector", lambda v, h=h, g_tab=g_tab, bb_tab=bb_tab: v.tensor_scalar(
                                out=qn_out[:], in0=qn_out[:], scalar1=g_tab[:, h:h + 1],
                                scalar2=bb_tab[:, h:h + 1], op0=OP.mult, op1=OP.add))
                            p.op("vector", lambda v, shi=shi: v.tensor_copy(out=qs_out[:, shi], in_=qn_out[:]))
                            av = p.op("vector", lambda v, shi=shi, slo=slo: v.tensor_sub(
                                out=qs_out[:, slo], in0=qn_out[:], in1=qs_out[:, shi]), ch=cV)
                            p.wait("gpsimd", cV, av)
                            spill[gi] = p.dma("gpsimd", PSP, lambda g, ti=ti, h=h, so=so, dst_sp=dst_sp: g.dma_start(
                                out=dst_sp[ti, h], in_=qs_out[:, so * 2 * TT:(so + 1) * 2 * TT]))

                q_toks = wtok_h["q"]
                k_toks = wtok_h["k"]
                proj_ln(p, 0, q_toks, bq_r, gq_t, bbq_t, qn_sp)
                q_last = last_w_use[0]
                v_toks = load_w(p, 2, 0, gate=q_last)
                proj_ln(p, 1, k_toks, bk_r, gk_t, bbk_t, kn_sp)

                for tok in v_toks:
                    p.wait("tensor", tok[0], tok[1])
                wofs = 0
                vevac = {}
                vspill = {}
                vnv_v = {}
                for tt in range(16):
                    if tt > 0:
                        p.wait("tensor", cA, vevac[tt - 1])
                    for half in range(2):
                        hsl = slice(half * TT, (half + 1) * TT)
                        for ci in range(NCH):
                            xh = slice(ci * L + tt * 128, ci * L + tt * 128 + 128)
                            whi = slice(wofs + ci * 2 * C + half * TT, wofs + ci * 2 * C + half * TT + TT)
                            wlo = slice(wofs + ci * 2 * C + C + half * TT, wofs + ci * 2 * C + C + half * TT + TT)
                            p.op("tensor", lambda t, hsl=hsl, xh=xh, whi=whi, ci=ci: t.matmul(
                                ps_v[:, hsl], xnh_sb[:, xh], w_sb[:, whi], start=(ci == 0), stop=False))
                            p.op("tensor", lambda t, hsl=hsl, xh=xh, wlo=wlo: t.matmul(
                                ps_v[:, hsl], xnh_sb[:, xh], w_sb[:, wlo], start=False, stop=False))
                            p.op("tensor", lambda t, hsl=hsl, xh=xh, whi=whi: t.matmul(
                                ps_v[:, hsl], xnl_sb[:, xh], w_sb[:, whi], start=False, stop=False))
                        vmm = p.op("tensor", lambda t, hsl=hsl: t.matmul(
                            ps_v[:, hsl], onesrow[0:1, 0:128], bv_r[0:1, hsl], start=False, stop=True),
                            ch=(cT if half == 1 else None))
                    v_sb = v_sbs[tt % 2]
                    v2s = v2ss[tt % 2]
                    p.wait("scalar", cT, vmm)
                    if tt > 1:
                        p.wait("scalar", cV, vnv_v[tt - 2])
                    elif tt == 0:
                        p.wait("scalar", cV, apply_v[gbase[0] - 1])
                    vevac[tt] = p.op("scalar", lambda s, v_sb=v_sb: s.copy(out=v_sb[:], in_=ps_v[:]), ch=cA)
                    sqval = p.op("scalar", lambda s, v_sb=v_sb, v2s=v2s: s.activation(out=v2s[:], in_=v_sb[:], func=AF.Square), ch=cA)
                    p.wait("vector", cA, sqval)
                    vseq(p, [lambda v, v_sb=v_sb: v.reduce_sum(out=s1c[:], in_=v_sb[:], axis=AX.X),
                             lambda v, v2s=v2s: v.reduce_sum(out=s2c[:], in_=v2s[:], axis=AX.X),
                             lambda v: v.tensor_scalar_mul(out=c_mu[:], in0=s1c[:], scalar1=1.0 / C),
                             lambda v: v.tensor_mul(out=c_t0[:], in0=c_mu[:], in1=c_mu[:]),
                             lambda v: v.tensor_scalar(out=c_a[:], in0=s2c[:], scalar1=1.0 / C, scalar2=LEPS, op0=OP.mult, op1=OP.add),
                             lambda v: v.tensor_sub(out=c_a[:], in0=c_a[:], in1=c_t0[:])])
                    p.wait("scalar", cV)
                    p.op("scalar", lambda s: s.activation(out=c_t0[:], in_=c_a[:], func=AF.Sqrt), ch=cA)
                    p.wait("vector", cA)
                    vseq(p, [lambda v: v.reciprocal(out=c_r[:], in_=c_t0[:])])
                    p.wait("vector", cV)
                    p.op("vector", lambda v, v_sb=v_sb: v.tensor_scalar(out=v_sb[:], in0=v_sb[:], scalar1=c_mu[:], scalar2=c_r[:], op0=OP.subtract, op1=OP.mult))
                    p.op("vector", lambda v, v_sb=v_sb: v.tensor_mul(out=vnob[:], in0=v_sb[:], in1=bgv[:]))
                    p.op("vector", lambda v: v.tensor_add(out=vnob[:], in0=vnob[:], in1=bbvB[:]))
                    so = tt % 2
                    vhi = slice(so * 2 * C, so * 2 * C + C)
                    vlo = slice(so * 2 * C + C, so * 2 * C + 2 * C)
                    if tt >= 2 and vspill.get(tt - 2) is not None:
                        tok = vspill[tt - 2]
                        p.wait("vector", tok[0], tok[1])
                    p.op("vector", lambda v, vhi=vhi: v.tensor_copy(out=vns[:, vhi], in_=vnob[:]))
                    vnv = p.op("vector", lambda v, vhi=vhi, vlo=vlo: v.tensor_sub(out=vns[:, vlo], in0=vnob[:], in1=vns[:, vhi]), ch=cV)
                    vnv_v[tt] = vnv
                    p.wait("gpsimd", cV, vnv)
                    vspill[tt] = p.dma("gpsimd", PSP, lambda g, tt=tt, so=so: g.dma_start(
                        out=vn_sp[tt], in_=vns[:, so * 2 * C:(so + 1) * 2 * C]))
        _ph2()
        s_xn.close()

        SW = 256
        NSTR = L // SW
        def _ph3(b=b):
            with ExitStack() as ph:
                pc = ph.enter_context
                qn_sb = pc(nc.sbuf_tensor(f"qn_{b}", [128, NCH * NTT * 2 * TT], BF16))
                vn_sb = pc(nc.sbuf_tensor(f"vn_{b}", [128, 16 * 2 * C], BF16))
                kn_sb = pc(nc.sbuf_tensor(f"kn_{b}", [128, 2 * NCH * 2 * TT], BF16))
                s_sb = pc(nc.sbuf_tensor(f"ssb_{b}", [128, 2 * L], F32))
                at_sb = pc(nc.sbuf_tensor(f"at_{b}", [128, 16 * 2 * SW], BF16))
                o_buf = pc(nc.sbuf_tensor(f"ob_{b}", [128, 2 * SW], F32))
                nm_c = pc(nc.sbuf_tensor(f"nm_{b}", [128, 1], F32))
                sg_c = pc(nc.sbuf_tensor(f"sg_{b}", [128, 1], F32))
                rs_c = pc(nc.sbuf_tensor(f"rs_{b}", [128, 1], F32))
                ps_s = PSA[:, 0:2 * TT]
                ps_t = PSA[:, 2 * TT:4 * TT]
                ps_o = PSA[:, 4 * TT:6 * TT]

                p = PG
                quiesce(p)
                qn_toks = {}
                for ti in range(NTT):
                    toks = []
                    for h in range(NCH):
                        toks.append(p.dma("sync", PIN, lambda s, ti=ti, h=h: s.dma_start(
                            out=qn_sb[:, (h * NTT + ti) * 2 * TT: (h * NTT + ti + 1) * 2 * TT], in_=qn_sp[ti, h])))
                    qn_toks[ti] = toks
                vn_toks = []
                for tt in range(16):
                    vn_toks.append(p.dma("sync", PIN, lambda s, tt=tt: s.dma_start(
                        out=vn_sb[:, tt * 2 * C:(tt + 1) * 2 * C], in_=vn_sp[tt])))

                def load_kn(st_i):
                    kofs = (st_i % 2) * NCH * 2 * SW
                    toks = []
                    for h in range(NCH):
                        for piece in range(2):
                            toks.append(p.dma("sync", PIN, lambda s, st_i=st_i, h=h, piece=piece, kofs=kofs: s.dma_start(
                                out=kn_sb[:, kofs + h * 2 * SW + piece * SW: kofs + h * 2 * SW + (piece + 1) * SW],
                                in_=kn_sp[st_i // 2, h][:, piece * TT + (st_i % 2) * SW: piece * TT + (st_i % 2) * SW + SW])))
                    return toks

                kn_toks = {}
                smm = {}
                sev = {}
                norm_val = {}
                tp = {}
                tpev = {}
                ospill = {}
                oevac = {}
                last_score = None
                tp_emitted = set()

                def emit_transposes(gls):
                    if gls in tp_emitted:
                        return
                    tp_emitted.add(gls)
                    sbase = (gls % 2) * L
                    p.wait("tensor", cV, norm_val[gls])
                    for Lc in range(16):
                        tb = slice((Lc % 2) * TT, (Lc % 2) * TT + 128)
                        if (gls, Lc - 2) in tpev:
                            p.wait("tensor", cV, tpev[(gls, Lc - 2)])
                        elif Lc < 2 and (gls - 1, 14 + Lc) in tpev:
                            p.wait("tensor", cV, tpev[(gls - 1, 14 + Lc)])
                        tp[(gls, Lc)] = p.op("tensor", lambda t, tb=tb, Lc=Lc, sbase=sbase: t.transpose(
                            ps_t[:, tb], s_sb[:, sbase + Lc * 128: sbase + (Lc + 1) * 128], ident[:]), ch=cT)
                        p.wait("vector", cT, tp[(gls, Lc)])
                        ls = gls % 2
                        ahi = slice(Lc * 2 * SW + ls * 128, Lc * 2 * SW + ls * 128 + 128)
                        alo = slice(Lc * 2 * SW + SW + ls * 128, Lc * 2 * SW + SW + ls * 128 + 128)
                        p.op("vector", lambda v, ahi=ahi, tb=tb: v.tensor_copy(out=at_sb[:, ahi], in_=ps_t[:, tb]))
                        tpev[(gls, Lc)] = p.op("vector", lambda v, ahi=ahi, alo=alo, tb=tb: v.tensor_sub(
                            out=at_sb[:, alo], in0=ps_t[:, tb], in1=at_sb[:, ahi]), ch=cV)

                kn_toks[0] = load_kn(0)
                for st_i in range(NSTR):
                    for tok in kn_toks[st_i]:
                        p.wait("tensor", tok[0], tok[1])
                    kofs = (st_i % 2) * NCH * 2 * SW
                    for ls in range(SW // 128):
                        gls = st_i * 2 + ls
                        sbase = (gls % 2) * L
                        for Lt in range(NTT):
                            gmi = gls * 4 + Lt
                            pb = slice((Lt % 2) * TT, (Lt % 2) * TT + TT)
                            if gmi >= 2 and (gmi - 2) in sev:
                                p.wait("tensor", cA, sev[gmi - 2])
                            if st_i == 0 and ls == 0:
                                for tok in qn_toks[Lt]:
                                    p.wait("tensor", tok[0], tok[1])
                            for h in range(NCH):
                                khi = slice(kofs + h * 2 * SW + ls * 128, kofs + h * 2 * SW + ls * 128 + 128)
                                klo = slice(kofs + h * 2 * SW + SW + ls * 128, kofs + h * 2 * SW + SW + ls * 128 + 128)
                                qbase = (h * NTT + Lt) * 2 * TT
                                qhi = slice(qbase, qbase + TT)
                                qlo = slice(qbase + TT, qbase + 2 * TT)
                                p.op("tensor", lambda t, pb=pb, khi=khi, qhi=qhi, h=h: t.matmul(
                                    ps_s[:, pb], kn_sb[:, khi], qn_sb[:, qhi], start=(h == 0), stop=False))
                                p.op("tensor", lambda t, pb=pb, khi=khi, qlo=qlo: t.matmul(
                                    ps_s[:, pb], kn_sb[:, khi], qn_sb[:, qlo], start=False, stop=False))
                                smm[(gls, Lt)] = p.op("tensor", lambda t, pb=pb, klo=klo, qhi=qhi, h=h: t.matmul(
                                    ps_s[:, pb], kn_sb[:, klo], qn_sb[:, qhi], start=False, stop=(h == NCH - 1)),
                                    ch=(cT if h == NCH - 1 else None))
                            p.wait("scalar", cT, smm[(gls, Lt)])
                            if gls >= 2 and (gls - 2, 15) in tp:
                                p.wait("scalar", cT, tp[(gls - 2, 15)])
                            sev[gmi] = p.op("scalar", lambda s, Lt=Lt, pb=pb, sbase=sbase: s.activation(
                                out=s_sb[:, sbase + Lt * TT: sbase + (Lt + 1) * TT], in_=ps_s[:, pb], func=AF.Copy), ch=cA)
                        last_score = smm[(gls, 3)]
                        p.wait("vector", cA, sev[gls * 4 + 3])
                        nmv = p.op("vector", lambda v, sbase=sbase: v.reduce_max(out=nm_c[:], in_=s_sb[:, sbase:sbase + L], axis=AX.X, negate=True), ch=cV)
                        p.wait("scalar", cV, nmv)
                        ev = p.op("scalar", lambda s, sbase=sbase: s.activation(out=s_sb[:, sbase:sbase + L], in_=s_sb[:, sbase:sbase + L],
                                                                                func=AF.Exp, bias=nm_c[:]), ch=cA)
                        p.wait("vector", cA, ev)
                        vseq(p, [lambda v, sbase=sbase: v.reduce_sum(out=sg_c[:], in_=s_sb[:, sbase:sbase + L], axis=AX.X),
                                 lambda v: v.reciprocal(out=rs_c[:], in_=sg_c[:]),
                                 lambda v, sbase=sbase: v.tensor_scalar_mul(out=s_sb[:, sbase:sbase + L], in0=s_sb[:, sbase:sbase + L], scalar1=rs_c[:])])
                        norm_val[gls] = cV.n
                        if gls >= 1:
                            emit_transposes(gls - 1)
                    gls_last = st_i * 2 + 1
                    emit_transposes(gls_last)
                    if st_i + 1 < NSTR:
                        p.wait("sync", cT, last_score)
                        kn_toks[st_i + 1] = load_kn(st_i + 1)
                    p.wait("tensor", cV, tpev[(gls_last, 15)])
                    if st_i == 0:
                        for tok in vn_toks:
                            p.wait("tensor", tok[0], tok[1])
                    for cc in range(NCH):
                        pb = slice((cc % 2) * TT, (cc % 2) * TT + SW)
                        if cc >= 2:
                            p.wait("tensor", cV, oevac[cc - 2])
                        for Lc in range(16):
                            vbase = Lc * 2 * C
                            vhi = slice(vbase + cc * 128, vbase + cc * 128 + 128)
                            vlo = slice(vbase + C + cc * 128, vbase + C + cc * 128 + 128)
                            ahi = slice(Lc * 2 * SW, Lc * 2 * SW + SW)
                            alo = slice(Lc * 2 * SW + SW, Lc * 2 * SW + 2 * SW)
                            p.op("tensor", lambda t, pb=pb, vhi=vhi, ahi=ahi, Lc=Lc: t.matmul(
                                ps_o[:, pb], vn_sb[:, vhi], at_sb[:, ahi], start=(Lc == 0), stop=False))
                            p.op("tensor", lambda t, pb=pb, vhi=vhi, alo=alo: t.matmul(
                                ps_o[:, pb], vn_sb[:, vhi], at_sb[:, alo], start=False, stop=False))
                            avm = p.op("tensor", lambda t, pb=pb, vlo=vlo, ahi=ahi, Lc=Lc: t.matmul(
                                ps_o[:, pb], vn_sb[:, vlo], at_sb[:, ahi], start=False, stop=(Lc == 15)),
                                ch=(cT if Lc == 15 else None))
                        p.wait("vector", cT, avm)
                        ob = slice((cc % 2) * SW, (cc % 2) * SW + SW)
                        gcc = st_i * NCH + cc
                        if gcc >= 2 and ospill.get(gcc - 2) is not None:
                            tok = ospill[gcc - 2]
                            p.wait("vector", tok[0], tok[1])
                        oevac[cc] = p.op("vector", lambda v, ob=ob, pb=pb: v.tensor_copy(out=o_buf[:, ob], in_=ps_o[:, pb]), ch=cV)
                        p.wait("gpsimd", cV, oevac[cc])
                        ospill[gcc] = p.dma("gpsimd", PSP, lambda g, st_i=st_i, cc=cc, ob=ob: g.dma_start(
                            out=o_sp[st_i, cc], in_=o_buf[:, ob]))
    
        _ph3()
        def _ph4(b=b):
            with ExitStack() as ph:
                pc = ph.enter_context
                wpj_t = pc(nc.sbuf_tensor(f"wpj_{b}", [128, 2 * C], BF16))
                g_sb = pc(nc.sbuf_tensor(f"g_{b}", [128, NHC * TT], F32))
                wfct = pc(nc.sbuf_tensor(f"wfct_{b}", [128, 2 * NCH * 128], BF16))
                xnt = pc(nc.sbuf_tensor(f"xnt_{b}", [128, NCH * TT], F32))
                xnh = pc(nc.sbuf_tensor(f"xnh4_{b}", [128, 2 * NCH * TT], BF16))
                xnl = pc(nc.sbuf_tensor(f"xnl4_{b}", [128, 2 * NCH * TT], BF16))
                g2s = pc(nc.sbuf_tensor(f"g2s_{b}", [128, 2 * TT], F32))
                zs = pc(nc.sbuf_tensor(f"zs_{b}", [128, 2 * TT], F32))
                hn32 = pc(nc.sbuf_tensor(f"hn32_{b}", [128, 2 * TT], F32))
                hnh = pc(nc.sbuf_tensor(f"hnh_{b}", [128, 2 * TT], BF16))
                hnl = pc(nc.sbuf_tensor(f"hnl_{b}", [128, 2 * TT], BF16))
                br_t = pc(nc.sbuf_tensor(f"br4_{b}", [128, TT], F32))
                bmr_t = pc(nc.sbuf_tensor(f"bmr4_{b}", [128, TT], F32))
                sA = pc(nc.sbuf_tensor(f"sA4_{b}", [128, TT], F32))
                sB = pc(nc.sbuf_tensor(f"sB4_{b}", [128, TT], F32))
                o_t = pc(nc.sbuf_tensor(f"ot_{b}", [128, NCH * TT], F32))
                x_t = pc(nc.sbuf_tensor(f"xt4_{b}", [128, NCH * TT], F32))
                outb = pc(nc.sbuf_tensor(f"outb_{b}", [128, 2 * TT], F32))
                ps_all = PSA
                ps_fc = ps_all[:, 0:2 * TT]
                ps_st4 = ps_all[0:1, 2 * TT:4 * TT]
                ps_b4 = ps_all[:, 4 * TT:5 * TT]

                p = PG
                quiesce(p)
                fcmm = {}
                gel = {}
                sqv = {}
                st2 = {}
                pjmm = {}
                split_v = {}
                finev = {}
                out_tok = {}
                stats_free4 = None
                xn_toks = {}
                rec_tok = {}

                def load_xn(ti):
                    xb = (ti % 2) * NCH * TT
                    if ti >= 2:
                        p.wait("gpsimd", cT, fcmm[(ti - 2) * NHC + NHC - 1])
                        p.wait("gpsimd", cV, rec_tok[ti - 2])
                    toks = []
                    for ci in range(NCH):
                        toks.append(p.dma("gpsimd", PSP, lambda g, ci=ci, ti=ti, xb=xb: g.dma_start(
                            out=xnh[:, xb + ci * TT:xb + (ci + 1) * TT], in_=xnh_sp[ti, ci])))
                        toks.append(p.dma("gpsimd", PSP, lambda g, ci=ci, ti=ti, xb=xb: g.dma_start(
                            out=xnl[:, xb + ci * TT:xb + (ci + 1) * TT], in_=xnl_sp[ti, ci])))
                    xn_toks[ti] = toks

                load_xn(0)
                load_xn(1)
                for ti in range(NTT):
                    xb = (ti % 2) * NCH * TT
                    for tok in xn_toks[ti]:
                        p.wait("vector", tok[0], tok[1])
                        p.wait("tensor", tok[0], tok[1])
                    if ti > 0:
                        p.wait("vector", cT, pjmm[(ti - 1, NHC - 1)])
                        p.wait("tensor", cV, finev[(ti - 1) * NCH + 1])
                    for ci in range(NCH):
                        sl = slice(xb + ci * TT, xb + (ci + 1) * TT)
                        rec = p.op("vector", lambda v, sl=sl, ci=ci: v.tensor_add(
                            out=xnt[:, ci * TT:(ci + 1) * TT], in0=xnh[:, sl], in1=xnl[:, sl]), ch=cV)
                    rec_tok[ti] = rec

                    for h in range(NHC):
                        gi = ti * NHC + h
                        wb = (h % 2) * NCH * 128
                        if gi >= 2:
                            p.wait("sync", cT, fcmm[gi - 2])
                        wtok = p.dma("sync", PIN, lambda s, h=h, wb=wb: s.dma_start(
                            out=wfct[:, wb:wb + NCH * 128], in_=wfcq_sp[h]))
                        pb = slice((h % 2) * TT, (h % 2) * TT + TT)
                        if gi >= 2:
                            p.wait("tensor", cA, gel[gi - 2])
                        p.wait("tensor", wtok[0], wtok[1])
                        for ci in range(NCH):
                            wsl = slice(wb + ci * 128, wb + ci * 128 + 128)
                            p.op("tensor", lambda t, pb=pb, wsl=wsl, ci=ci, xb=xb: t.matmul(
                                ps_fc[:, pb], wfct[:, wsl], xnh[:, xb + ci * TT:xb + (ci + 1) * TT], start=(ci == 0), stop=False))
                            fcmm[gi] = p.op("tensor", lambda t, pb=pb, wsl=wsl, ci=ci, xb=xb: t.matmul(
                                ps_fc[:, pb], wfct[:, wsl], xnl[:, xb + ci * TT:xb + (ci + 1) * TT], start=False,
                                stop=(ci == NCH - 1)), ch=(cT if ci == NCH - 1 else None))
                        if h >= 1:
                            gprev = gi - 1
                            hp = h - 1
                            if hp == 0 and stats_free4 is not None:
                                p.wait("tensor", cV, stats_free4)
                            if hp == 0 and ti > 0:
                                p.wait("tensor", cV, finev[(ti - 1) * NCH + 3])
                            p.wait("tensor", cA, gel[gprev])
                            p.op("tensor", lambda t, hp=hp: t.matmul(
                                ps_st4[0:1, 0:TT], onesc[:, 0:1], g_sb[:, hp * TT:(hp + 1) * TT], start=(hp == 0), stop=(hp == NHC - 1)))
                            p.wait("tensor", cA, sqv[gprev])
                            st2[gprev] = p.op("tensor", lambda t, gprev=gprev: t.matmul(
                                ps_st4[0:1, TT:2 * TT], onesc[:, 0:1], g2s[:, (gprev % 2) * TT:(gprev % 2) * TT + TT],
                                start=(gprev % NHC == 0), stop=(gprev % NHC == NHC - 1)), ch=cT)
                        p.wait("scalar", cT, fcmm[gi])
                        gsl = slice(h * TT, (h + 1) * TT)
                        gel[gi] = p.op("scalar", lambda s, gsl=gsl, pb=pb: s.activation(
                            out=g_sb[:, gsl], in_=ps_fc[:, pb], func=AF.Gelu), ch=cA)
                        g2b = slice((gi % 2) * TT, (gi % 2) * TT + TT)
                        if gi >= 2:
                            p.wait("scalar", cT, st2[gi - 2])
                        sqv[gi] = p.op("scalar", lambda s, gsl=gsl, g2b=g2b: s.activation(
                            out=g2s[:, g2b], in_=g_sb[:, gsl], func=AF.Square), ch=cA)
                    if ti + 2 < NTT:
                        load_xn(ti + 2)
                    gi = ti * NHC + NHC - 1
                    p.wait("tensor", cA, gel[gi])
                    p.op("tensor", lambda t: t.matmul(
                        ps_st4[0:1, 0:TT], onesc[:, 0:1], g_sb[:, (NHC - 1) * TT:NHC * TT], start=False, stop=True))
                    p.wait("tensor", cA, sqv[gi])
                    st2[gi] = p.op("tensor", lambda t, gi=gi: t.matmul(
                        ps_st4[0:1, TT:2 * TT], onesc[:, 0:1], g2s[:, (gi % 2) * TT:(gi % 2) * TT + TT],
                        start=False, stop=True), ch=cT)
                    p.wait("vector", cT, st2[gi])
                    if ti > 0:
                        p.wait("tensor", cV, finev[(ti - 1) * NCH + NCH - 1])
                    stats_free4 = ln_bcast(p, ps_st4, ps_b4, HF, br_t, bmr_t, sA, sB, nr=False)
                    p.wait("vector", cV)

                    ftoks = {}
                    for ci in range(NCH):
                        obuf = slice(ci * TT, (ci + 1) * TT)
                        if ti > 0:
                            p.wait("sync", cV, finev[(ti - 1) * NCH + ci])
                        f1 = p.dma("sync", PIN, lambda s, ci=ci, ti=ti, obuf=obuf: s.dma_start(
                            out=o_t[:, obuf.start:obuf.start + 256], in_=o_sp[2 * ti, ci]))
                        f2 = p.dma("sync", PIN, lambda s, ci=ci, ti=ti, obuf=obuf: s.dma_start(
                            out=o_t[:, obuf.start + 256:obuf.start + 512], in_=o_sp[2 * ti + 1, ci]))
                        f3 = p.dma("sync", PIN, lambda s, ci=ci, ti=ti, obuf=obuf: s.dma_start(
                            out=x_t[:, obuf], in_=xin[b, ci * 128:(ci + 1) * 128, ti * TT:(ti + 1) * TT]))
                        ftoks[ci] = (f1, f2, f3)
                    ptoks = {}
                    for h in range(2):
                        if ti > 0:
                            p.wait("sync", cT, pjmm[(ti - 1, NHC - 2 + h)])
                        ptoks[h] = p.dma("sync", PIN, lambda s, h=h: s.dma_start(
                            out=wpj_t[:, (h % 2) * C:(h % 2) * C + C], in_=wprojq_sp[h]))
                    for h in range(NHC):
                        hb = slice((h % 2) * TT, (h % 2) * TT + TT)
                        wpb = (h % 2) * C
                        if h >= 2:
                            p.wait("sync", cT, pjmm[(ti, h - 2)])
                            ptoks[h] = p.dma("sync", PIN, lambda s, h=h, wpb=wpb: s.dma_start(
                                out=wpj_t[:, wpb:wpb + C], in_=wprojq_sp[h]))
                        ptok = ptoks[h]
                        gsl = slice(h * TT, (h + 1) * TT)
                        if h >= 2:
                            p.wait("vector", cT, pjmm[(ti, h - 2)])
                        p.op("vector", lambda v, gsl=gsl, hb=hb: v.tensor_mul(out=zs[:, hb], in0=g_sb[:, gsl], in1=br_t[:]))
                        p.op("vector", lambda v, hb=hb: v.tensor_sub(out=zs[:, hb], in0=zs[:, hb], in1=bmr_t[:]))
                        p.op("vector", lambda v, h=h, hb=hb: v.tensor_scalar(
                            out=hn32[:, hb], in0=zs[:, hb], scalar1=lnw_t[:, h:h + 1],
                            scalar2=lnb_t[:, h:h + 1], op0=OP.mult, op1=OP.add))
                        p.op("vector", lambda v, hb=hb: v.tensor_copy(out=hnh[:, hb], in_=hn32[:, hb]))
                        split_v[h] = p.op("vector", lambda v, hb=hb: v.tensor_sub(out=hnl[:, hb], in0=hn32[:, hb], in1=hnh[:, hb]), ch=cV)
                        p.wait("tensor", cV, split_v[h])
                        p.wait("tensor", ptok[0], ptok[1])
                        for cc in range(NCH):
                            wsl = slice(wpb + cc * 128, wpb + cc * 128 + 128)
                            fsl = slice(cc * TT, (cc + 1) * TT)
                            p.op("tensor", lambda t, wsl=wsl, fsl=fsl, h=h, hb=hb: t.matmul(
                                ps_all[:, fsl], wpj_t[:, wsl], hnh[:, hb], start=(h == 0), stop=False))
                            pjmm[(ti, h)] = p.op("tensor", lambda t, wsl=wsl, fsl=fsl, h=h, cc=cc, hb=hb: t.matmul(
                                ps_all[:, fsl], wpj_t[:, wsl], hnl[:, hb], start=False, stop=(h == NHC - 1)),
                                ch=(cT if cc == NCH - 1 else None))

                    p.wait("vector", cT, pjmm[(ti, NHC - 1)])
                    for ci in range(NCH):
                        gfi = ti * NCH + ci
                        obuf = slice(ci * TT, (ci + 1) * TT)
                        osl = slice((ci % 2) * TT, (ci % 2) * TT + TT)
                        for tok in ftoks[ci]:
                            p.wait("vector", tok[0], tok[1])
                        fsl = slice(ci * TT, (ci + 1) * TT)
                        if gfi >= 2 and out_tok.get(gfi - 2) is not None:
                            tok = out_tok[gfi - 2]
                            p.wait("vector", tok[0], tok[1])
                        p.op("vector", lambda v, fsl=fsl, obuf=obuf, osl=osl: v.tensor_add(out=outb[:, osl], in0=ps_all[:, fsl], in1=o_t[:, obuf]))
                        p.op("vector", lambda v, obuf=obuf, osl=osl: v.tensor_add(out=outb[:, osl], in0=outb[:, osl], in1=x_t[:, obuf]))
                        finev[gfi] = p.op("vector", lambda v, osl=osl, ci=ci: v.tensor_add(
                            out=outb[:, osl], in0=outb[:, osl], in1=xnt[:, ci * TT:(ci + 1) * TT]), ch=cV)
                        p.wait("sync", cV, finev[gfi])
                        out_tok[gfi] = p.dma("sync", POUT, lambda s, ci=ci, ti=ti, osl=osl: s.dma_start(
                            out=outT[b, ci * 128:(ci + 1) * 128, ti * TT:(ti + 1) * TT], in_=outb[:, osl]))
    
        _ph4()
    PG.wait_dma("sync", POUT)
    PG.run()

    stack.close()
    return nc


_NC_CACHE = None


def kernel(**inputs):
    global _NC_CACHE
    x = np.asarray(inputs["L0_attn_input_sequence"])
    wq_a = np.ascontiguousarray(np.asarray(inputs["L0_attn_Wq"], dtype=np.float32))
    wk_a = np.ascontiguousarray(np.asarray(inputs["L0_attn_Wk"], dtype=np.float32))
    wv_a = np.ascontiguousarray(np.asarray(inputs["L0_attn_Wv"], dtype=np.float32))
    wfcT_a = np.ascontiguousarray(np.asarray(inputs["L0_ffn_W_fc"]).T)
    wprojT_a = np.ascontiguousarray(np.asarray(inputs["L0_ffn_W_proj"]).T)

    def row(v):
        return np.ascontiguousarray(np.asarray(v, dtype=np.float32).reshape(1, -1))

    def tab(v, nch):
        return np.ascontiguousarray(np.asarray(v, dtype=np.float32).reshape(nch, 128).T)

    common = {
        "wq": wq_a, "wk": wk_a, "wv": wv_a,
        "bq": row(inputs["L0_attn_bq"]), "bk": row(inputs["L0_attn_bk"]), "bv": row(inputs["L0_attn_bv"]),
        "gq": tab(inputs["L0_attn_gq"], NCH), "gk": tab(inputs["L0_attn_gk"], NCH),
        "betaq": tab(inputs["L0_attn_betaq"], NCH), "betak": tab(inputs["L0_attn_betak"], NCH),
        "gv": row(inputs["L0_attn_gv"]), "betav": row(inputs["L0_attn_betav"]),
        "wfcT": wfcT_a, "wprojT": wprojT_a,
        "lnw": tab(inputs["L0_ffn_ln_w"], NHC), "lnb": tab(inputs["L0_ffn_ln_b"], NHC),
        "ident": np.eye(128, dtype=np.float32),
        "onesc": np.ascontiguousarray(np.stack([np.ones(128), np.zeros(128)], axis=1).astype(np.float32)),
        "onesc2": np.ascontiguousarray(np.stack([np.zeros(128), np.ones(128)], axis=1).astype(np.float32)),
        "onesrow": np.ones((1, TT), dtype=np.float32),
    }

    if _NC_CACHE is None:
        _NC_CACHE = build_nc()
    nc = _NC_CACHE

    in_maps = []
    for c in range(NCORES):
        xT = np.ascontiguousarray(x[c * BPC:(c + 1) * BPC].transpose(0, 2, 1))
        m = dict(common)
        m["xT"] = xT
        in_maps.append(m)

    res = run_bass_kernel_spmd(nc, in_maps, core_ids=list(range(NCORES)))
    out = np.concatenate(
        [np.asarray(r["outT"]).transpose(0, 2, 1) for r in res.results], axis=0)
    return np.ascontiguousarray(out, dtype=np.float32)


if __name__ == "__main__":
    import importlib.util as _il
    spec = _il.spec_from_file_location("_ref", "/root/problem/reference.py")
    ref = _il.module_from_spec(spec)
    spec.loader.exec_module(ref)
    ins = {k: np.asarray(v) for k, v in ref.setup_inputs().items()}
    got = kernel(**ins)
    exp = np.asarray(ref.reference(**ins))
    err = np.linalg.norm(got - exp) / np.linalg.norm(exp)
    print("rel_err:", err)
